# version 03 (final)
# baseline (speedup 1.0000x reference)
import contextlib
import numpy as np
import ml_dtypes
import concourse.bass as bass
import concourse.mybir as mybir
from concourse.bass_utils import run_bass_kernel_spmd

F32 = mybir.dt.float32
BF16 = mybir.dt.bfloat16
AF = mybir.ActivationFunctionType
ALU = mybir.AluOpType
AX = mybir.AxisListType
BF = ml_dtypes.bfloat16

P = 128
TC = 512
NCH = 8
NCO = 4
DM = 1024
KD = DM // P
DI = 2048
CI = DI // P
DH = 1024
DHC = DH // P
NST = 16
EXP_NEG10 = float(np.exp(np.float32(-10.0)))
DBG = False
SKIP_CW = False
NSC = NST * DHC


class Sem:
    def __init__(self, ctx, nc, name):
        self.h = ctx.enter_context(nc.semaphore(name))



FRONT_SEMS = ("wld wldg xld preS preV lnm lbm sqr ln1 ln2 ln3 lnb xln ipm exi zm "
              "ezs cvv ecv xpm expq dtm edt bcm ebc dA sc ym gtq ygo").split()


def emit_front(nc, block, S, io, layer, pfx):
    clip = (layer == 1)
    x_in = io["x"]
    w_in = io["w_in"]
    w_cv = io["w_cv"]
    b_cv = io["b_cv"]
    w_x = io["w_x"]
    w_dt = io["w_dt"]
    b_dt = io["b_dt"]
    a_log = io["a_log"]
    d_vec = io["d_vec"]
    yg_out = io["yg"]

    ident16 = nc.inline_tensor(np.eye(P, dtype=np.float32).astype(BF), name=pfx + "ident16")
    ones_c = nc.inline_tensor(np.ones((P, 1), np.float32), name=pfx + "ones_c")
    ones_r = nc.inline_tensor(np.ones((1, P), np.float32), name=pfx + "ones_r")
    ones_r16 = nc.inline_tensor(np.ones((1, P), np.float32).astype(BF), name=pfx + "ones_r16")
    cvals = nc.inline_tensor(np.tile(np.array([[1e-5, 1.0]], np.float32), (P, 1)), name=pfx + "cvals")
    selm = np.zeros((96, 32 * P), np.float32)
    for j in range(32):
        selm[64 + j, j * P:(j + 1) * P] = 1.0
    sel_bc = nc.inline_tensor(selm.astype(BF), name=pfx + "sel_bc")

    with contextlib.ExitStack() as ctx:
        E = ctx.enter_context
        xb = E(nc.sbuf_tensor(pfx + "xb", [P, 2 * KD * TC], F32))
        xln = E(nc.sbuf_tensor(pfx + "xln", [P, KD * TC], BF16))
        rows = E(nc.sbuf_tensor(pfx + "rows", [1, 5 * TC], F32))
        wi16 = E(nc.sbuf_tensor(pfx + "wi16", [P, KD * 3072], BF16))
        wcv = E(nc.sbuf_tensor(pfx + "wcv", [P, CI * 4], F32))
        cvacc = E(nc.sbuf_tensor(pfx + "cvacc", [P, 2 * TC], F32))
        sq = cvacc
        cb = E(nc.sbuf_tensor(pfx + "cb", [P, CI], F32))
        wx16 = E(nc.sbuf_tensor(pfx + "wx16", [P, CI * 96], BF16))
        wdt16 = E(nc.sbuf_tensor(pfx + "wdt16", [64, DH], BF16))
        bdt = E(nc.sbuf_tensor(pfx + "bdt", [P, DHC], F32))
        acoef = E(nc.sbuf_tensor(pfx + "acoef", [P, DHC * NST], F32))
        dvec = E(nc.sbuf_tensor(pfx + "dvec", [P, DHC], F32))
        xi16 = E(nc.sbuf_tensor(pfx + "xi16", [P, CI * (TC + 3)], BF16))
        u16 = E(nc.sbuf_tensor(pfx + "u16", [P, DHC * TC], BF16))
        u2 = E(nc.sbuf_tensor(pfx + "u2", [P, TC], BF16))
        zs2 = E(nc.sbuf_tensor(pfx + "zs2", [P, TC], BF16))
        xdbl = E(nc.sbuf_tensor(pfx + "xdbl", [96, TC], BF16))
        dtsb = E(nc.sbuf_tensor(pfx + "dtsb", [P, DHC * TC], F32))
        bc16 = E(nc.sbuf_tensor(pfx + "bc16", [P, 2 * NST * TC], BF16))
        xdt = E(nc.sbuf_tensor(pfx + "xdt", [P, TC], F32))
        dab = E(nc.sbuf_tensor(pfx + "dab", [P, 2 * TC], F32))
        xdbb = E(nc.sbuf_tensor(pfx + "xdbb", [P, TC], F32))
        hb = E(nc.sbuf_tensor(pfx + "hb", [P, TC], F32))
        hc16 = E(nc.sbuf_tensor(pfx + "hc16", [P, 2 * TC], BF16))
        state = E(nc.sbuf_tensor(pfx + "state", [P, DHC * NST], F32))
        gt = E(nc.sbuf_tensor(pfx + "gt", [P, TC], F32))
        ygb = E(nc.sbuf_tensor(pfx + "ygb", [P, TC], BF16))
        sid = E(nc.sbuf_tensor(pfx + "sid", [P, P], BF16))
        soc = E(nc.sbuf_tensor(pfx + "soc", [P, 1], F32))
        sor = E(nc.sbuf_tensor(pfx + "sor", [1, P], F32))
        sor16 = E(nc.sbuf_tensor(pfx + "sor16", [1, P], BF16))
        cst = E(nc.sbuf_tensor(pfx + "cst", [P, 2], F32))
        ssel = E(nc.sbuf_tensor(pfx + "ssel", [96, 32 * P], BF16))

        pLa = E(nc.psum_tensor(pfx + "pLa", [P, TC], F32))
        pLb = E(nc.psum_tensor(pfx + "pLb", [P, TC], F32))
        pB0 = E(nc.psum_tensor(pfx + "pB0", [P, TC], F32))
        pB1 = E(nc.psum_tensor(pfx + "pB1", [P, TC], F32))
        pM0 = E(nc.psum_tensor(pfx + "pM0", [P, TC], F32))
        pM1 = E(nc.psum_tensor(pfx + "pM1", [P, TC], F32))
        pY0 = E(nc.psum_tensor(pfx + "pY0", [P, TC], F32))
        pY1 = E(nc.psum_tensor(pfx + "pY1", [P, TC], F32))
        pM = [pM0, pM1]
        pB = [pB0, pB1]
        pY = [pY0, pY1]
        pZ = [pLa, pLb]

        @block.sync
        def _(sync):
            def ld(dst, src):
                sync.dma_start(out=dst, in_=src).then_inc(S["wld"].h, 16)
            ld(soc[:], ones_c.ap())
            ld(sor[:], ones_r.ap())
            ld(cb[:], b_cv.ap())
            ld(wcv[:], w_cv.ap())
            ld(bdt[:], b_dt.ap())
            ld(dvec[:], d_vec.ap())
            ld(acoef[:], a_log.ap())
            ld(cst[:], cvals.ap())
            x3 = x_in.ap().rearrange("(kc p) t -> p kc t", p=P)
            for tc in range(min(2, NCH)):
                dst = xb[:, (tc % 2) * KD * TC:(tc % 2 + 1) * KD * TC].rearrange(
                    "p (kc t) -> p kc t", kc=KD)
                sync.dma_start(out=dst, in_=x3[:, :, tc * TC:(tc + 1) * TC]).then_inc(S["xld"].h, 16)
            for tc in range(NCH):
                if tc + 2 < NCH:
                    sync.wait_ge(S["xln"].h, KD * (tc + 1))
                    sync.wait_ge(S["edt"].h, DHC * (tc + 1))
                    dst = xb[:, (tc % 2) * KD * TC:(tc % 2 + 1) * KD * TC].rearrange(
                        "p (kc t) -> p kc t", kc=KD)
                    sync.dma_start(out=dst,
                                   in_=x3[:, :, (tc + 2) * TC:(tc + 3) * TC]).then_inc(S["xld"].h, 16)
                for dc in range(DHC):
                    sync.wait_ge(S["gtq"].h, DHC * tc + dc + 1)
                    sync.dma_start(
                        out=yg_out.ap()[dc * P:(dc + 1) * P, tc * TC:(tc + 1) * TC],
                        in_=ygb[:]).then_inc(S["ygo"].h, 16)
            sync.wait_ge(S["ygo"].h, 16 * DHC * NCH)

        @block.gpsimd
        def _(g):
            g.dma_start(out=wi16[:].rearrange("p (kc m) -> p kc m", kc=KD),
                        in_=w_in.ap().rearrange("(kc p) m -> p kc m", p=P)).then_inc(S["wldg"].h, 16)
            g.dma_start(out=wx16[:].rearrange("p (cc m) -> p cc m", cc=CI),
                        in_=w_x.ap().rearrange("(cc p) m -> p cc m", p=P)).then_inc(S["wldg"].h, 16)
            g.dma_start(out=wdt16[:], in_=w_dt.ap()).then_inc(S["wldg"].h, 16)
            g.dma_start(out=sid[:], in_=ident16.ap()).then_inc(S["wldg"].h, 16)
            g.dma_start(out=sor16[:], in_=ones_r16.ap()).then_inc(S["wldg"].h, 16)
            g.dma_start(out=ssel[:], in_=sel_bc.ap()).then_inc(S["wldg"].h, 16)

        NW = 16 * 8
        NWG = 16 * 6

        @block.scalar
        def _(scalar):
            scalar.wait_ge(S["wld"].h, NW)
            scalar.wait_ge(S["wldg"].h, NWG)
            scalar.activation(acoef[:], acoef[:], AF.Exp).then_inc(S["preS"].h, 1)
            scalar.wait_ge(S["preV"].h, 1)
            for tc in range(NCH):
                scalar.wait_ge(S["xld"].h, 16 * (tc + 1))
                for kc in range(KD):
                    if KD * tc + kc >= 2:
                        scalar.wait_ge(S["lbm"].h, KD * tc + kc - 1)
                    xs = xb[:, ((tc % 2) * KD + kc) * TC:((tc % 2) * KD + kc + 1) * TC]
                    scalar.activation(sq[:, (kc % 2) * TC:(kc % 2 + 1) * TC], xs,
                                      AF.Square).then_inc(S["sqr"].h, 1)
                scalar.wait_ge(S["ln1"].h, tc + 1)
                scalar.activation(rows[:, 3 * TC:4 * TC], rows[:, 2 * TC:3 * TC],
                                  AF.Ln, bias=cst[0:1, 0:1])
                scalar.activation(rows[:, 3 * TC:4 * TC], rows[:, 3 * TC:4 * TC],
                                  AF.Exp, scale=-0.5).then_inc(S["ln2"].h, 1)
                scalar.wait_ge(S["gtq"].h, DHC * tc)
                for cc in range(CI):
                    scalar.wait_ge(S["cvv"].h, CI * tc + cc + 1)
                    if cc < DHC:
                        dst = u16[:, cc * TC:(cc + 1) * TC]
                        scalar.wait_ge(S["xpm"].h, CI * tc)
                    else:
                        dst = u2[:]
                        scalar.wait_ge(S["xpm"].h,
                                       CI * tc + cc if cc >= 9 else CI * tc)
                    scalar.activation(dst, cvacc[:, (cc % 2) * TC:(cc % 2 + 1) * TC],
                                      AF.Silu, bias=cb[:, cc:cc + 1]).then_inc(S["ecv"].h, 1)
                scalar.wait_ge(S["sc"].h, NSC * tc)
                sftmp = xb[:, (tc % 2) * KD * TC:(tc % 2) * KD * TC + TC]
                for dc in range(DHC):
                    scalar.wait_ge(S["dtm"].h, DHC * tc + dc + 1)
                    scalar.activation(sftmp, pM[dc % 2][:], AF.Exp, bias=bdt[:, dc:dc + 1])
                    scalar.activation(dtsb[:, dc * TC:(dc + 1) * TC], sftmp, AF.Ln,
                                      bias=cst[:, 1:2]).then_inc(S["edt"].h, 1)
                for dc in range(DHC):
                    scalar.wait_ge(S["edt"].h, DHC * tc + dc + 1)
                    for n in range(NST):
                        idx = NSC * tc + NST * dc + n
                        if idx >= 2:
                            scalar.wait_ge(S["sc"].h, idx - 1)
                        scalar.activation(
                            dab[:, (idx % 2) * TC:(idx % 2 + 1) * TC],
                            dtsb[:, dc * TC:(dc + 1) * TC], AF.Exp,
                            scale=acoef[:, dc * NST + n:dc * NST + n + 1]).then_inc(S["dA"].h, 1)
                    scalar.wait_ge(S["zm"].h, DHC * tc + dc + 1)
                    if DHC * tc + dc >= 1:
                        scalar.wait_ge(S["gtq"].h, DHC * tc + dc)
                    scalar.activation(zs2[:],
                                      pZ[dc % 2][:], AF.Silu).then_inc(S["ezs"].h, 1)

        @block.tensor
        def _(tensor):
            tensor.wait_ge(S["wld"].h, NW)
            tensor.wait_ge(S["wldg"].h, NWG)
            tensor.wait_ge(S["preV"].h, 1)
            for tc in range(NCH):
                xoff = (tc % 2) * KD * TC
                tensor.wait_ge(S["ezs"].h, DHC * tc)
                tensor.wait_ge(S["xld"].h, 16 * (tc + 1))
                for kc in range(KD):
                    tensor.matmul(pLa[:1, :], soc[:],
                                  xb[:, xoff + kc * TC:xoff + (kc + 1) * TC],
                                  start=(kc == 0), stop=(kc == KD - 1))
                for kc in range(KD):
                    tensor.wait_ge(S["sqr"].h, KD * tc + kc + 1)
                    tensor.matmul(pLb[:1, :], soc[:],
                                  sq[:, (kc % 2) * TC:(kc % 2 + 1) * TC],
                                  start=(kc == 0),
                                  stop=(kc == KD - 1)).then_inc(S["lbm"].h, 1)
                tensor.wait_ge(S["ln3"].h, tc + 1)
                tensor.wait_ge(S["ebc"].h, 2 * NST * tc)
                tensor.wait_ge(S["expq"].h, tc)
                tensor.matmul(pB0[:], sor[:], rows[:, 3 * TC:4 * TC], start=True, stop=True)
                tensor.matmul(pB1[:], sor[:], rows[:, 4 * TC:5 * TC],
                              start=True, stop=True).then_inc(S["lnb"].h, 1)
                tensor.wait_ge(S["edt"].h, DHC * tc)
                tensor.wait_ge(S["xln"].h, KD * (tc + 1))
                for cc in range(CI):
                    if CI * tc + cc >= 2:
                        tensor.wait_ge(S["exi"].h, CI * tc + cc - 1)
                    for kc in range(KD):
                        ins = tensor.matmul(
                            pM[cc % 2][:], wi16[:, kc * 3072 + cc * P:kc * 3072 + (cc + 1) * P],
                            xln[:, kc * TC:(kc + 1) * TC],
                            start=(kc == 0), stop=(kc == KD - 1))
                    ins.then_inc(S["ipm"].h, 1)
                tensor.wait_ge(S["xln"].h, KD * (tc + 1))
                for cc in range(CI):
                    tensor.wait_ge(S["ecv"].h, CI * tc + cc + 1)
                    src = (u16[:, cc * TC:(cc + 1) * TC] if cc < DHC
                           else u2[:])
                    tensor.matmul(pB0[:96, :], wx16[:, cc * 96:(cc + 1) * 96], src,
                                  start=(cc == 0), stop=(cc == CI - 1)).then_inc(S["xpm"].h, 1)
                tensor.wait_ge(S["exi"].h, CI * (tc + 1))
                tensor.wait_ge(S["expq"].h, tc + 1)
                for dc in range(DHC):
                    if DHC * tc + dc >= 2:
                        tensor.wait_ge(S["edt"].h, DHC * tc + dc - 1)
                    tensor.matmul(pM[dc % 2][:], wdt16[:, dc * P:(dc + 1) * P],
                                  xdbl[0:64, :], start=True,
                                  stop=True).then_inc(S["dtm"].h, 1)
                for j in range(2 * NST):
                    if 2 * NST * tc + j >= 2:
                        tensor.wait_ge(S["ebc"].h, 2 * NST * tc + j - 1)
                    tensor.matmul(pB[j % 2][:], ssel[:, j * P:(j + 1) * P], xdbl[:],
                                  start=True, stop=True).then_inc(S["bcm"].h, 1)
                tensor.wait_ge(S["ln1"].h, tc + 1)
                for dc in range(DHC):
                    if DHC * tc + dc >= 2:
                        tensor.wait_ge(S["ezs"].h, DHC * tc + dc - 1)
                    for kc in range(KD):
                        ins = tensor.matmul(
                            pZ[dc % 2][:],
                            wi16[:, kc * 3072 + (CI + dc) * P:kc * 3072 + (CI + dc + 1) * P],
                            xln[:, kc * TC:(kc + 1) * TC],
                            start=(kc == 0), stop=(kc == KD - 1))
                    ins.then_inc(S["zm"].h, 1)
                    if DHC * tc + dc >= 2:
                        tensor.wait_ge(S["gtq"].h, DHC * tc + dc - 1)
                    for n in range(NST):
                        idx = NSC * tc + NST * dc + n
                        tensor.wait_ge(S["sc"].h, idx + 1)
                        tensor.matmul(pY[dc % 2][:], sid[:],
                                      hc16[:, (idx % 2) * TC:(idx % 2 + 1) * TC],
                                      start=(n == 0),
                                      stop=(n == NST - 1)).then_inc(S["ym"].h, 1)

        @block.vector
        def _(vector):
            vector.wait_ge(S["preS"].h, 1)
            vector.tensor_scalar(acoef[:], acoef[:], -1.0, None, op0=ALU.mult)
            vector.memset(state[:], 0.0)
            for cc in range(CI):
                vector.memset(xi16[:, cc * (TC + 3):cc * (TC + 3) + 3], 0.0)
            vector.nop().then_inc(S["preV"].h, 1)
            for tc in range(NCH):
                vector.wait_ge(S["lbm"].h, KD * (tc + 1))
                vector.tensor_scalar(rows[:, 0:TC], pLa[:1, :], 1.0 / DM, None, op0=ALU.mult)
                vector.tensor_scalar(rows[:, TC:2 * TC], pLb[:1, :], 1.0 / DM, None, op0=ALU.mult)
                vector.tensor_mul(gt[0:1, :], rows[:, 0:TC], rows[:, 0:TC])
                vector.tensor_sub(rows[:, 2 * TC:3 * TC], rows[:, TC:2 * TC],
                                  gt[0:1, :]).then_inc(S["ln1"].h, 1)
                vector.wait_ge(S["ln2"].h, tc + 1)
                vector.tensor_mul(rows[:, 4 * TC:5 * TC], rows[:, 0:TC],
                                  rows[:, 3 * TC:4 * TC]).then_inc(S["ln3"].h, 1)
                vector.wait_ge(S["lnb"].h, tc + 1)
                vector.wait_ge(S["ipm"].h, CI * tc)
                vector.wait_ge(S["zm"].h, DHC * tc)
                xoff = (tc % 2) * KD * TC
                for kc in range(KD):
                    vector.tensor_mul(gt[:], xb[:, xoff + kc * TC:xoff + (kc + 1) * TC], pB0[:])
                    vector.tensor_sub(xln[:, kc * TC:(kc + 1) * TC], gt[:],
                                      pB1[:]).then_inc(S["xln"].h, 1)
                for cc in range(CI):
                    vector.wait_ge(S["ipm"].h, CI * tc + cc + 1)
                    vector.tensor_copy(
                        xi16[:, cc * (TC + 3) + 3:cc * (TC + 3) + 3 + TC],
                        pM[cc % 2][:]).then_inc(S["exi"].h, 1)
                for cc in range(CI):
                    if CI * tc + cc >= 2:
                        vector.wait_ge(S["ecv"].h, CI * tc + cc - 1)
                    acc = cvacc[:, (cc % 2) * TC:(cc % 2 + 1) * TC]
                    base = cc * (TC + 3)
                    vector.tensor_scalar(acc, xi16[:, base:base + TC],
                                         wcv[:, cc * 4:cc * 4 + 1], None, op0=ALU.mult)
                    for k in range(1, 4):
                        vector.scalar_tensor_tensor(acc, xi16[:, base + k:base + k + TC],
                                                    wcv[:, cc * 4 + k:cc * 4 + k + 1],
                                                    acc, op0=ALU.mult, op1=ALU.add)
                    vector.tensor_copy(xi16[:, base:base + 3],
                                       xi16[:, base + TC:base + TC + 3]).then_inc(S["cvv"].h, 1)
                vector.wait_ge(S["xpm"].h, CI * (tc + 1))
                vector.wait_ge(S["dtm"].h, DHC * tc)
                vector.wait_ge(S["bcm"].h, 2 * NST * tc)
                vector.tensor_copy(xdbl[:], pB0[:96, :]).then_inc(S["expq"].h, 1)
                for j in range(2 * NST):
                    vector.wait_ge(S["bcm"].h, 2 * NST * tc + j + 1)
                    vector.tensor_copy(bc16[:, j * TC:(j + 1) * TC],
                                       pB[j % 2][:]).then_inc(S["ebc"].h, 1)
                for dc in range(DHC):
                    vector.wait_ge(S["edt"].h, DHC * tc + dc + 1)
                    vector.tensor_mul(xdt[:], dtsb[:, dc * TC:(dc + 1) * TC],
                                      u16[:, dc * TC:(dc + 1) * TC])
                    for n in range(NST):
                        idx = NSC * tc + NST * dc + n
                        da = dab[:, (idx % 2) * TC:(idx % 2 + 1) * TC]
                        vector.wait_ge(S["dA"].h, idx + 1)
                        if clip:
                            vector.tensor_scalar(da, da, EXP_NEG10, None, op0=ALU.max)
                        vector.tensor_mul(xdbb[:], xdt[:], bc16[:, n * TC:(n + 1) * TC])
                        vector.tensor_tensor_scan(
                            hb[:], da, xdbb[:],
                            state[:, dc * NST + n:dc * NST + n + 1],
                            op0=ALU.mult, op1=ALU.add)
                        vector.tensor_copy(state[:, dc * NST + n:dc * NST + n + 1],
                                           hb[:, TC - 1:TC])
                        if idx >= 2:
                            vector.wait_ge(S["ym"].h, idx - 1)
                        vector.tensor_mul(
                            hc16[:, (idx % 2) * TC:(idx % 2 + 1) * TC], hb[:],
                            bc16[:, (NST + n) * TC:(NST + n + 1) * TC]).then_inc(S["sc"].h, 1)
                    vector.wait_ge(S["ym"].h, NSC * tc + NST * (dc + 1))
                    vector.wait_ge(S["ezs"].h, DHC * tc + dc + 1)
                    if DHC * tc + dc >= 1:
                        vector.wait_ge(S["ygo"].h, 16 * (DHC * tc + dc))
                    vector.scalar_tensor_tensor(gt[:], u16[:, dc * TC:(dc + 1) * TC],
                                                dvec[:, dc:dc + 1], pY[dc % 2][:],
                                                op0=ALU.mult, op1=ALU.add)
                    vector.tensor_mul(ygb[:], gt[:],
                                      zs2[:]).then_inc(S["gtq"].h, 1)


def build_front(layer):
    nc = bass.Bass("TRN2", target_bir_lowering=False)
    io = dict(
        x=nc.dram_tensor("x", [DM, 4096], F32, kind="ExternalInput"),
        w_in=nc.dram_tensor("w_in", [DM, 3072], F32, kind="ExternalInput"),
        w_cv=nc.dram_tensor("w_cv", [P, CI * 4], F32, kind="ExternalInput"),
        b_cv=nc.dram_tensor("b_cv", [P, CI], F32, kind="ExternalInput"),
        w_x=nc.dram_tensor("w_x", [DI, 96], F32, kind="ExternalInput"),
        w_dt=nc.dram_tensor("w_dt", [64, DH], F32, kind="ExternalInput"),
        b_dt=nc.dram_tensor("b_dt", [P, DHC], F32, kind="ExternalInput"),
        a_log=nc.dram_tensor("a_log", [P, DHC * NST], F32, kind="ExternalInput"),
        d_vec=nc.dram_tensor("d_vec", [P, DHC], F32, kind="ExternalInput"),
        yg=nc.dram_tensor("yg", [DH, 4096], BF16, kind="ExternalOutput"),
    )
    with contextlib.ExitStack() as ctx:
        S = {n: Sem(ctx, nc, n) for n in FRONT_SEMS}
        block = ctx.enter_context(nc.Block())
        emit_front(nc, block, S, io, layer, "f%d_" % layer)
    return nc


def _front_inputs(ins, layer, h):
    if layer == 0:
        g = lambda k: np.asarray(ins["L0_attn_" + k])
        w_in_full = g("in_proj_w")
        conv_w = g("conv_w")[:, 0, :].T
        b_conv, w_x = g("conv_b"), g("x_proj_w")
        w_dtf, b_dtf = g("dt_proj_w"), g("dt_proj_b")
        a_logf, d_f = g("A_log"), g("D")
    else:
        g = lambda k: np.asarray(ins["L1_attn_" + k])
        w_in_full = g("W_in")
        conv_w = g("W_conv")
        b_conv, w_x = g("b_conv"), g("W_xproj")
        w_dtf, b_dtf = g("W_dt"), g("b_dt")
        a_logf, d_f = g("A_log"), g("D")
    own = np.arange(h * DH, (h + 1) * DH)
    perm = np.r_[own, np.arange((1 - h) * DH, (2 - h) * DH)]
    w_in = np.concatenate([w_in_full[:, :DI][:, perm], w_in_full[:, DI:][:, own]], axis=1)
    cw = conv_w[:, perm]
    w_cv = np.zeros((P, CI * 4), np.float32)
    for cc in range(CI):
        for k in range(4):
            w_cv[:, cc * 4 + k] = cw[k, cc * P:(cc + 1) * P]
    return dict(
        w_in=np.ascontiguousarray(w_in, np.float32),
        w_cv=w_cv,
        b_cv=np.ascontiguousarray(b_conv[perm].reshape(CI, P).T, np.float32),
        w_x=np.ascontiguousarray(w_x[perm], np.float32),
        w_dt=np.ascontiguousarray(w_dtf[:, own], np.float32),
        b_dt=np.ascontiguousarray(b_dtf[own].reshape(DHC, P).T, np.float32),
        a_log=np.ascontiguousarray(
            a_logf[own].reshape(DHC, P, NST).transpose(1, 0, 2).reshape(P, DHC * NST),
            np.float32),
        d_vec=np.ascontiguousarray(d_f[own].reshape(DHC, P).T, np.float32),
    )




BACK0_SEMS = ("wldh wldg xld yld sq0 lb0 l01 l02 l03 l0b xlf nfm fgm fge mx "
              "x1d sq1 lb1 l11 l12 l13 l1b xl1 xl1d gm g1 g2 g3 g4 g5 "
              "xmh wle hm he cwb cwe hsc wdm ma x1ld x2e x2d gg").split()


def emit_back0(nc, block, S, io, pfx):
    x_in = io["x"]
    y_in = io["yf"]
    w_out = io["w_out"]
    fg_w = io["fg_w"]
    fg_b = io["fg_b"]
    wg = io["wg"]
    wu = io["wu"]
    wd = io["wd"]
    x2_out = io["x2"]
    x1_dram = io["x1_scr"]
    xl1_dram = io["xl1_scr"]
    cw_dbg = io.get("cw_dbg") or nc.dram_tensor(pfx + "cw_dbg", [8, 2048], BF16, kind="Internal")
    gw_dbg = cw_dbg

    ident16 = nc.inline_tensor(np.eye(P, dtype=np.float32).astype(BF), name=pfx + "ident16")
    ones_c = nc.inline_tensor(np.ones((P, 1), np.float32), name=pfx + "ones_c")
    ones_r = nc.inline_tensor(np.ones((1, P), np.float32), name=pfx + "ones_r")
    cvals = nc.inline_tensor(np.tile(np.array([[1e-5, 1.0]], np.float32), (P, 1)), name=pfx + "cvals")
    sel8m = np.zeros((8, 8 * P), np.float32)
    for e in range(8):
        sel8m[e, e * P:(e + 1) * P] = 1.0
    sel8 = nc.inline_tensor(sel8m.astype(BF), name=pfx + "sel8")
    ones8 = nc.inline_tensor(np.ones((8, P), np.float32).astype(BF), name=pfx + "ones8")

    NT = 2048
    with contextlib.ExitStack() as ctx:
        E = ctx.enter_context
        rows = E(nc.sbuf_tensor(pfx + "rows", [1, 5 * TC], F32))
        sqcv = E(nc.sbuf_tensor(pfx + "sqcv", [P, 2 * TC], F32))
        gt = E(nc.sbuf_tensor(pfx + "gt", [P, TC], F32))
        gw = E(nc.sbuf_tensor(pfx + "gw", [P, 32], F32))
        cw16s = E(nc.sbuf_tensor(pfx + "cw16s", [P, 8], BF16))
        sid = E(nc.sbuf_tensor(pfx + "sid", [P, P], BF16))
        soc = E(nc.sbuf_tensor(pfx + "soc", [P, 1], F32))
        sor = E(nc.sbuf_tensor(pfx + "sor", [1, P], F32))
        cst = E(nc.sbuf_tensor(pfx + "cst", [P, 2], F32))
        ssel8 = E(nc.sbuf_tensor(pfx + "ssel8", [8, 8 * P], BF16))
        sone8 = E(nc.sbuf_tensor(pfx + "sone8", [8, P], BF16))
        cwT = E(nc.sbuf_tensor(pfx + "cwT", [8, NT], BF16))
        wg16 = E(nc.sbuf_tensor(pfx + "wg16", [P, KD * 8], BF16))
        pLa = E(nc.psum_tensor(pfx + "pLa", [P, TC], F32))
        pLb = E(nc.psum_tensor(pfx + "pLb", [P, TC], F32))
        pB0 = E(nc.psum_tensor(pfx + "pB0", [P, TC], F32))
        pB1 = E(nc.psum_tensor(pfx + "pB1", [P, TC], F32))
        pM0 = E(nc.psum_tensor(pfx + "pM0", [P, TC], F32))
        pM1 = E(nc.psum_tensor(pfx + "pM1", [P, TC], F32))
        pG = E(nc.psum_tensor(pfx + "pG", [P, 8], F32))
        pT = E(nc.psum_tensor(pfx + "pT", [8, P], BF16))
        pM = [pM0, pM1]
        pL = [pLa, pLb]
        pB = [pB0, pB1]
        with contextlib.ExitStack() as pctx:
            PE_ = pctx.enter_context
            xb = PE_(nc.sbuf_tensor(pfx + "xb", [P, 2 * KD * TC], F32))
            y16 = PE_(nc.sbuf_tensor(pfx + "y16", [P, 2 * CI * TC], BF16))
            wo16 = PE_(nc.sbuf_tensor(pfx + "wo16", [P, CI * DM], BF16))
            fg16 = PE_(nc.sbuf_tensor(pfx + "fg16", [P, KD * DM], BF16))
            fgb = PE_(nc.sbuf_tensor(pfx + "fgb", [P, KD], F32))
            xlnf = PE_(nc.sbuf_tensor(pfx + "xlnf", [P, KD * TC], F32))
            xln16 = PE_(nc.sbuf_tensor(pfx + "xln16", [P, KD * TC], BF16))
            x1c = PE_(nc.sbuf_tensor(pfx + "x1c", [P, KD * TC], F32))
            xl116 = PE_(nc.sbuf_tensor(pfx + "xl116", [P, KD * TC], BF16))
            fgs = PE_(nc.sbuf_tensor(pfx + "fgs", [P, TC], F32))

            @block.sync
            def _(sync):
                def ld(dst, src):
                    sync.dma_start(out=dst, in_=src).then_inc(S["wldh"].h, 16)
                ld(soc[:], ones_c.ap())
                ld(sor[:], ones_r.ap())
                ld(cst[:], cvals.ap())
                ld(fgb[:], fg_b.ap())
                x3 = x_in.ap().rearrange("(kc p) t -> p kc t", p=P)
                y3 = y_in.ap().rearrange("(kc p) t -> p kc t", p=P)
                for tc in range(min(2, NCO)):
                    dx = xb[:, (tc % 2) * KD * TC:(tc % 2 + 1) * KD * TC].rearrange(
                        "p (kc t) -> p kc t", kc=KD)
                    sync.dma_start(out=dx, in_=x3[:, :, tc * TC:(tc + 1) * TC]).then_inc(S["xld"].h, 16)
                    dy = y16[:, (tc % 2) * CI * TC:(tc % 2 + 1) * CI * TC].rearrange(
                        "p (kc t) -> p kc t", kc=CI)
                    sync.dma_start(out=dy, in_=y3[:, :, tc * TC:(tc + 1) * TC]).then_inc(S["yld"].h, 16)
                for tc in range(NCO):
                    if tc + 2 < NCO:
                        sync.wait_ge(S["mx"].h, KD * (tc + 1))
                        dx = xb[:, (tc % 2) * KD * TC:(tc % 2 + 1) * KD * TC].rearrange(
                            "p (kc t) -> p kc t", kc=KD)
                        sync.dma_start(out=dx,
                                       in_=x3[:, :, (tc + 2) * TC:(tc + 3) * TC]).then_inc(S["xld"].h, 16)
                        sync.wait_ge(S["nfm"].h, KD * (tc + 1))
                        dy = y16[:, (tc % 2) * CI * TC:(tc % 2 + 1) * CI * TC].rearrange(
                            "p (kc t) -> p kc t", kc=CI)
                        sync.dma_start(out=dy,
                                       in_=y3[:, :, (tc + 2) * TC:(tc + 3) * TC]).then_inc(S["yld"].h, 16)
                    sync.wait_ge(S["mx"].h, KD * (tc + 1))
                    sync.dma_start(
                        out=x1_dram.ap().rearrange("(kc p) t -> p kc t", p=P)[:, :, tc * TC:(tc + 1) * TC],
                        in_=x1c[:].rearrange("p (kc t) -> p kc t", kc=KD)).then_inc(S["x1d"].h, 16)
                    sync.wait_ge(S["xl1"].h, KD * (tc + 1))
                    sync.dma_start(
                        out=xl1_dram.ap().rearrange("(kc p) t -> p kc t", p=P)[:, :, tc * TC:(tc + 1) * TC],
                        in_=xl116[:].rearrange("p (kc t) -> p kc t", kc=KD)).then_inc(S["xl1d"].h, 16)

            @block.gpsimd
            def _(g):
                g.dma_start(out=wo16[:].rearrange("p (kc m) -> p kc m", kc=CI),
                            in_=w_out.ap().rearrange("(kc p) m -> p kc m", p=P)).then_inc(S["wldg"].h, 16)
                g.dma_start(out=fg16[:].rearrange("p (kc m) -> p kc m", kc=KD),
                            in_=fg_w.ap().rearrange("(kc p) m -> p kc m", p=P)).then_inc(S["wldg"].h, 16)
                g.dma_start(out=wg16[:].rearrange("p (kc m) -> p kc m", kc=KD),
                            in_=wg.ap().rearrange("(kc p) m -> p kc m", p=P)).then_inc(S["wldg"].h, 16)
                g.dma_start(out=sid[:], in_=ident16.ap()).then_inc(S["wldg"].h, 16)
                g.dma_start(out=ssel8[:], in_=sel8.ap()).then_inc(S["wldg"].h, 16)
                g.dma_start(out=sone8[:], in_=ones8.ap()).then_inc(S["wldg"].h, 16)
                for tc in range(NCO):
                    for tt in range(4):
                        gi = 4 * tc + tt
                        base = 12 * gi
                        g.wait_ge(S["gg"].h, base + 1)
                        g.tensor_scalar(gw[:, 8:9], gw[:, 30:31], -1.0, None,
                                        op0=ALU.mult).then_inc(S["gg"].h, 1)
                        g.wait_ge(S["gg"].h, base + 3)
                        g.tensor_scalar(gw[:, 9:17], gw[:, 0:8], 1.0, None,
                                        op0=ALU.is_lt).then_inc(S["gg"].h, 1)
                        g.tensor_mul(gw[:, 19:27], gw[:, 0:8],
                                     gw[:, 9:17]).then_inc(S["gg"].h, 1)
                        g.wait_ge(S["gg"].h, base + 6)
                        g.tensor_scalar(gw[:, 9:17], gw[:, 0:8], gw[:, 17:18], None,
                                        op0=ALU.is_ge).then_inc(S["gg"].h, 1)
                        g.tensor_mul(gw[:, 19:27], gw[:, 0:8],
                                     gw[:, 9:17]).then_inc(S["gg"].h, 1)
                        g.wait_ge(S["gg"].h, base + 9)
                        g.tensor_copy(gw[:, 28:29], gw[:, 27:28]).then_inc(S["gg"].h, 1)
                        g.wait_ge(S["gg"].h, base + 11)
                        if gi >= 1:
                            g.wait_ge(S["g4"].h, gi)
                        g.tensor_scalar(cw16s[:], gw[:, 19:27], gw[:, 29:30], None,
                                        op0=ALU.mult).then_inc(S["gg"].h, 1)

            @block.scalar
            def _(scalar):
                scalar.wait_ge(S["wldh"].h, 64)
                scalar.wait_ge(S["wldg"].h, 96)
                for tc in range(NCO):
                    scalar.wait_ge(S["xld"].h, 16 * (tc + 1))
                    for kc in range(KD):
                        if KD * tc + kc >= 2:
                            scalar.wait_ge(S["lb0"].h, KD * tc + kc - 1)
                        xs = xb[:, ((tc % 2) * KD + kc) * TC:((tc % 2) * KD + kc + 1) * TC]
                        scalar.activation(sqcv[:, (kc % 2) * TC:(kc % 2 + 1) * TC], xs,
                                          AF.Square).then_inc(S["sq0"].h, 1)
                    scalar.wait_ge(S["l01"].h, tc + 1)
                    scalar.activation(rows[:, 3 * TC:4 * TC], rows[:, 2 * TC:3 * TC],
                                      AF.Ln, bias=cst[0:1, 0:1])
                    scalar.activation(rows[:, 3 * TC:4 * TC], rows[:, 3 * TC:4 * TC],
                                      AF.Exp, scale=-0.5).then_inc(S["l02"].h, 1)
                    for m in range(KD):
                        scalar.wait_ge(S["fgm"].h, KD * tc + m + 1)
                        if KD * tc + m >= 1:
                            scalar.wait_ge(S["mx"].h, KD * tc + m)
                        scalar.activation(fgs[:], pL[m % 2][:], AF.Sigmoid,
                                          bias=fgb[:, m:m + 1]).then_inc(S["fge"].h, 1)
                    for kc in range(KD):
                        scalar.wait_ge(S["mx"].h, KD * tc + kc + 1)
                        if KD * tc + kc >= 2:
                            scalar.wait_ge(S["lb1"].h, KD * tc + kc - 1)
                        scalar.activation(sqcv[:, (kc % 2) * TC:(kc % 2 + 1) * TC],
                                          x1c[:, kc * TC:(kc + 1) * TC],
                                          AF.Square).then_inc(S["sq1"].h, 1)
                    scalar.wait_ge(S["l11"].h, tc + 1)
                    scalar.activation(rows[:, 3 * TC:4 * TC], rows[:, 2 * TC:3 * TC],
                                      AF.Ln, bias=cst[0:1, 0:1])
                    scalar.activation(rows[:, 3 * TC:4 * TC], rows[:, 3 * TC:4 * TC],
                                      AF.Exp, scale=-0.5).then_inc(S["l12"].h, 1)
                    for tt in range(4):
                        gi = 4 * tc + tt
                        scalar.wait_ge(S["gg"].h, 12 * gi + 2)
                        scalar.activation(gw[:, 0:8], pG[:], AF.Exp,
                                          bias=gw[:, 8:9]).then_inc(S["gg"].h, 1)

            @block.tensor
            def _(tensor):
                tensor.wait_ge(S["wldh"].h, 64)
                tensor.wait_ge(S["wldg"].h, 96)
                for tc in range(NCO):
                    xoff = (tc % 2) * KD * TC
                    yoff = (tc % 2) * CI * TC
                    tensor.wait_ge(S["fge"].h, KD * tc)
                    tensor.wait_ge(S["xld"].h, 16 * (tc + 1))
                    for kc in range(KD):
                        tensor.matmul(pLa[:1, :], soc[:],
                                      xb[:, xoff + kc * TC:xoff + (kc + 1) * TC],
                                      start=(kc == 0), stop=(kc == KD - 1))
                    for kc in range(KD):
                        tensor.wait_ge(S["sq0"].h, KD * tc + kc + 1)
                        tensor.matmul(pLb[:1, :], soc[:],
                                      sqcv[:, (kc % 2) * TC:(kc % 2 + 1) * TC],
                                      start=(kc == 0),
                                      stop=(kc == KD - 1)).then_inc(S["lb0"].h, 1)
                    tensor.wait_ge(S["l03"].h, tc + 1)
                    tensor.wait_ge(S["xl1"].h, KD * tc)
                    tensor.matmul(pB0[:], sor[:], rows[:, 3 * TC:4 * TC], start=True, stop=True)
                    tensor.matmul(pB1[:], sor[:], rows[:, 4 * TC:5 * TC],
                                  start=True, stop=True).then_inc(S["l0b"].h, 1)
                    tensor.wait_ge(S["xlf"].h, KD * (tc + 1))
                    tensor.wait_ge(S["yld"].h, 16 * (tc + 1))
                    for m in range(KD):
                        if KD * tc + m >= 2:
                            tensor.wait_ge(S["mx"].h, KD * tc + m - 1)
                        for kc in range(CI):
                            ins = tensor.matmul(
                                pM[m % 2][:], wo16[:, kc * DM + m * P:kc * DM + (m + 1) * P],
                                y16[:, yoff + kc * TC:yoff + (kc + 1) * TC],
                                start=(kc == 0), stop=(kc == CI - 1))
                        ins.then_inc(S["nfm"].h, 1)
                        if KD * tc + m >= 2:
                            tensor.wait_ge(S["fge"].h, KD * tc + m - 1)
                        for kc in range(KD):
                            ins = tensor.matmul(
                                pL[m % 2][:], fg16[:, kc * DM + m * P:kc * DM + (m + 1) * P],
                                xln16[:, kc * TC:(kc + 1) * TC],
                                start=(kc == 0), stop=(kc == KD - 1))
                        ins.then_inc(S["fgm"].h, 1)
                    tensor.wait_ge(S["fge"].h, KD * (tc + 1))
                    for kc in range(KD):
                        tensor.wait_ge(S["mx"].h, KD * tc + kc + 1)
                        tensor.matmul(pLa[:1, :], soc[:], x1c[:, kc * TC:(kc + 1) * TC],
                                      start=(kc == 0), stop=(kc == KD - 1))
                    for kc in range(KD):
                        tensor.wait_ge(S["sq1"].h, KD * tc + kc + 1)
                        tensor.matmul(pLb[:1, :], soc[:],
                                      sqcv[:, (kc % 2) * TC:(kc % 2 + 1) * TC],
                                      start=(kc == 0),
                                      stop=(kc == KD - 1)).then_inc(S["lb1"].h, 1)
                    tensor.wait_ge(S["l13"].h, tc + 1)
                    tensor.wait_ge(S["xlf"].h, KD * (tc + 1))
                    tensor.matmul(pB0[:], sor[:], rows[:, 3 * TC:4 * TC], start=True, stop=True)
                    tensor.matmul(pB1[:], sor[:], rows[:, 4 * TC:5 * TC],
                                  start=True, stop=True).then_inc(S["l1b"].h, 1)
                    tensor.wait_ge(S["xl1"].h, KD * (tc + 1))
                    for tt in range(4):
                        gi = 4 * tc + tt
                        if gi >= 1:
                            tensor.wait_ge(S["gg"].h, 12 * (gi - 1) + 3)
                        for kc in range(KD):
                            ins = tensor.matmul(
                                pG[:], xl116[:, kc * TC + tt * P:kc * TC + (tt + 1) * P],
                                wg16[:, kc * 8:(kc + 1) * 8],
                                start=(kc == 0), stop=(kc == KD - 1))
                        ins.then_inc(S["gm"].h, 1)
                        tensor.wait_ge(S["gg"].h, 12 * (gi + 1))
                        if gi >= 1:
                            tensor.wait_ge(S["g5"].h, gi)
                        tensor.transpose(pT[:], cw16s[:], sid[:]).then_inc(S["g4"].h, 1)

            @block.vector
            def _(vector):
                vector.wait_ge(S["wldh"].h, 64)
                for tc in range(NCO):
                    xoff = (tc % 2) * KD * TC
                    vector.wait_ge(S["lb0"].h, KD * (tc + 1))
                    vector.tensor_scalar(rows[:, 0:TC], pLa[:1, :], 1.0 / DM, None, op0=ALU.mult)
                    vector.tensor_scalar(rows[:, TC:2 * TC], pLb[:1, :], 1.0 / DM, None, op0=ALU.mult)
                    vector.tensor_mul(gt[0:1, :], rows[:, 0:TC], rows[:, 0:TC])
                    vector.tensor_sub(rows[:, 2 * TC:3 * TC], rows[:, TC:2 * TC],
                                      gt[0:1, :]).then_inc(S["l01"].h, 1)
                    vector.wait_ge(S["l02"].h, tc + 1)
                    vector.tensor_mul(rows[:, 4 * TC:5 * TC], rows[:, 0:TC],
                                      rows[:, 3 * TC:4 * TC]).then_inc(S["l03"].h, 1)
                    vector.wait_ge(S["l0b"].h, tc + 1)
                    vector.wait_ge(S["fgm"].h, KD * tc)
                    for kc in range(KD):
                        vector.tensor_mul(gt[:], xb[:, xoff + kc * TC:xoff + (kc + 1) * TC], pB0[:])
                        vector.tensor_sub(xlnf[:, kc * TC:(kc + 1) * TC], gt[:], pB1[:])
                        vector.tensor_copy(xln16[:, kc * TC:(kc + 1) * TC],
                                           xlnf[:, kc * TC:(kc + 1) * TC]).then_inc(S["xlf"].h, 1)
                    if tc >= 1:
                        vector.wait_ge(S["x1d"].h, 16 * tc)
                    for m in range(KD):
                        vector.wait_ge(S["nfm"].h, KD * tc + m + 1)
                        vector.wait_ge(S["fge"].h, KD * tc + m + 1)
                        vector.tensor_sub(gt[:], xlnf[:, m * TC:(m + 1) * TC], pM[m % 2][:])
                        vector.tensor_mul(gt[:], gt[:], fgs[:])
                        vector.tensor_add(gt[:], gt[:], pM[m % 2][:])
                        vector.tensor_add(x1c[:, m * TC:(m + 1) * TC], gt[:],
                                          xb[:, xoff + m * TC:xoff + (m + 1) * TC]).then_inc(S["mx"].h, 1)
                    vector.wait_ge(S["lb1"].h, KD * (tc + 1))
                    vector.tensor_scalar(rows[:, 0:TC], pLa[:1, :], 1.0 / DM, None, op0=ALU.mult)
                    vector.tensor_scalar(rows[:, TC:2 * TC], pLb[:1, :], 1.0 / DM, None, op0=ALU.mult)
                    vector.tensor_mul(gt[0:1, :], rows[:, 0:TC], rows[:, 0:TC])
                    vector.tensor_sub(rows[:, 2 * TC:3 * TC], rows[:, TC:2 * TC],
                                      gt[0:1, :]).then_inc(S["l11"].h, 1)
                    vector.wait_ge(S["l12"].h, tc + 1)
                    vector.tensor_mul(rows[:, 4 * TC:5 * TC], rows[:, 0:TC],
                                      rows[:, 3 * TC:4 * TC]).then_inc(S["l13"].h, 1)
                    vector.wait_ge(S["l1b"].h, tc + 1)
                    if tc >= 1:
                        vector.wait_ge(S["xl1d"].h, 16 * tc)
                    vector.wait_ge(S["gm"].h, 4 * tc)
                    for kc in range(KD):
                        vector.tensor_mul(gt[:], x1c[:, kc * TC:(kc + 1) * TC], pB0[:])
                        vector.tensor_sub(xl116[:, kc * TC:(kc + 1) * TC], gt[:],
                                          pB1[:]).then_inc(S["xl1"].h, 1)
                    for tt in range(4):
                        gi = 4 * tc + tt
                        base = 12 * gi
                        vector.wait_ge(S["gm"].h, gi + 1)
                        vector.reduce_max(gw[:, 30:31], pG[:], axis=AX.X).then_inc(S["gg"].h, 1)
                        vector.wait_ge(S["gg"].h, base + 5)
                        vector.reduce_max(gw[:, 17:18], gw[:, 19:27], axis=AX.X).then_inc(S["gg"].h, 1)
                        vector.wait_ge(S["gg"].h, base + 8)
                        vector.reduce_sum(gw[:, 27:28], gw[:, 19:27], axis=AX.X).then_inc(S["gg"].h, 1)
                        vector.wait_ge(S["gg"].h, base + 10)
                        vector.reciprocal(gw[:, 29:30], gw[:, 28:29]).then_inc(S["gg"].h, 1)
                        vector.wait_ge(S["g4"].h, gi + 1)
                        vector.tensor_copy(cwT[:, gi * P:(gi + 1) * P],
                                           pT[:]).then_inc(S["g5"].h, 1)

        with contextlib.ExitStack() as pctx:
            PE_ = pctx.enter_context
            wu16 = PE_(nc.sbuf_tensor(pfx + "wu16", [P, KD * DI], BF16, unsafe_allow_realloc=True))
            wd16 = PE_(nc.sbuf_tensor(pfx + "wd16", [P, CI * DM], BF16, unsafe_allow_realloc=True))
            xl1mh = PE_(nc.sbuf_tensor(pfx + "xl1mh", [P, KD * 1024], BF16, unsafe_allow_realloc=True))
            h16 = PE_(nc.sbuf_tensor(pfx + "h16", [P, CI * TC], BF16, unsafe_allow_realloc=True))
            cwr16 = PE_(nc.sbuf_tensor(pfx + "cwr16", [P, TC], BF16, unsafe_allow_realloc=True))
            macc = PE_(nc.sbuf_tensor(pfx + "macc", [P, KD * 1024], F32, unsafe_allow_realloc=True))
            x1s = PE_(nc.sbuf_tensor(pfx + "x1s", [P, 2 * TC], F32, unsafe_allow_realloc=True))

            NE = 9
            xl3 = xl1_dram.ap().rearrange("(kc p) t -> p kc t", p=P)
            x13 = x1_dram.ap().rearrange("(kc p) t -> p kc t", p=P)
            x23 = x2_out.ap().rearrange("(kc p) t -> p kc t", p=P)

            @block.sync
            def _(sync):
                sync.wait_ge(S["g4"].h, 16)
                sync.wait_ge(S["gg"].h, 192)
                sync.wait_ge(S["g5"].h, 16)
                if DBG:
                    sync.dma_start(out=cw_dbg.ap(), in_=cwT[:]).then_inc(S["x2d"].h, 16)
                for mh in range(2):
                    sync.wait_ge(S["xl1d"].h, 64)
                    if mh == 1:
                        sync.wait_ge(S["hm"].h, 2 * CI * 9)
                    sync.dma_start(
                        out=xl1mh[:].rearrange("p (kc t) -> p kc t", kc=KD),
                        in_=xl3[:, :, mh * 1024:(mh + 1) * 1024]).then_inc(S["xmh"].h, 16)
                    for dc in range(KD):
                        for tq in range(2):
                            i = 16 * mh + 2 * dc + tq
                            sync.wait_ge(S["x1d"].h, 64)
                            if i >= 2:
                                sync.wait_ge(S["x2e"].h, i - 1)
                            t0_ = mh * 1024 + tq * TC
                            sync.dma_start(
                                out=x1s[:, (i % 2) * TC:(i % 2 + 1) * TC],
                                in_=x13[:, dc, t0_:t0_ + TC]).then_inc(S["x1ld"].h, 16)
                            sync.wait_ge(S["x2e"].h, i + 1)
                            sync.dma_start(
                                out=x23[:, dc, t0_:t0_ + TC],
                                in_=gt[:] if False else x1s[:, (i % 2) * TC:(i % 2 + 1) * TC]
                            ).then_inc(S["x2d"].h, 16)
                sync.wait_ge(S["x2d"].h, 16 * 32 + (16 if DBG else 0))

            @block.gpsimd
            def _(g):
                g.wait_ge(S["g4"].h, 16)
                g.wait_ge(S["gg"].h, 192)
                g.wait_ge(S["g5"].h, 16)
                g.wait_ge(S["x1d"].h, 64)
                g.wait_ge(S["xl1d"].h, 64)
                for mh in range(2):
                    for e in range(NE):
                        i = NE * mh + e
                        if i >= 1:
                            g.wait_ge(S["hm"].h, 16 * 2 * i)
                            g.wait_ge(S["wdm"].h, KD * 2 * i)
                        g.dma_start(out=wu16[:].rearrange("p (kc m) -> p kc m", kc=KD),
                                    in_=wu.ap()[e].rearrange("(kc p) m -> p kc m", p=P)
                                    ).then_inc(S["wle"].h, 16)
                        g.dma_start(out=wd16[:].rearrange("p (kc m) -> p kc m", kc=CI),
                                    in_=wd.ap()[e].rearrange("(kc p) m -> p kc m", p=P)
                                    ).then_inc(S["wle"].h, 16)

            @block.tensor
            def _(tensor):
                tensor.wait_ge(S["gg"].h, 192)
                tensor.wait_ge(S["g5"].h, 16)
                tensor.wait_ge(S["x1d"].h, 64)
                tensor.wait_ge(S["xl1d"].h, 64)
                for mh in range(2):
                    tensor.wait_ge(S["xmh"].h, 16 * (mh + 1))
                    for e in range(NE):
                        i = NE * mh + e
                        tensor.wait_ge(S["wle"].h, 32 * (i + 1))
                        for tq in range(2):
                            j = 2 * i + tq
                            ej = 2 * (8 * mh + e) + tq
                            if e < 8 and not SKIP_CW:
                                if ej >= 1:
                                    tensor.wait_ge(S["cwe"].h, ej)
                                tensor.matmul(pLa[:], ssel8[:, e * P:(e + 1) * P],
                                              cwT[:, mh * 1024 + tq * TC:mh * 1024 + (tq + 1) * TC],
                                              start=True, stop=True).then_inc(S["cwb"].h, 1)
                            for fc in range(CI):
                                fi = CI * j + fc
                                if fi >= 2:
                                    tensor.wait_ge(S["he"].h, fi - 1)
                                for kc in range(KD):
                                    ins = tensor.matmul(
                                        pM[fc % 2][:], wu16[:, kc * DI + fc * P:kc * DI + (fc + 1) * P],
                                        xl1mh[:, kc * 1024 + tq * TC:kc * 1024 + (tq + 1) * TC],
                                        start=(kc == 0), stop=(kc == KD - 1))
                                ins.then_inc(S["hm"].h, 1)
                            for dc in range(KD):
                                di = KD * j + dc
                                if di >= 2:
                                    tensor.wait_ge(S["ma"].h, di - 1)
                                for fc in range(CI):
                                    hsrc = h16[:, fc * TC:(fc + 1) * TC]
                                    if e < 8 and not SKIP_CW:
                                        tensor.wait_ge(S["hsc"].h, CI * ej + fc + 1)
                                    else:
                                        tensor.wait_ge(S["he"].h, CI * j + fc + 1)
                                    ins = tensor.matmul(
                                        pB[dc % 2][:], wd16[:, fc * DM + dc * P:fc * DM + (dc + 1) * P],
                                        hsrc, start=(fc == 0), stop=(fc == CI - 1))
                                ins.then_inc(S["wdm"].h, 1)

            @block.scalar
            def _(scalar):
                scalar.wait_ge(S["g4"].h, 16)
                scalar.wait_ge(S["g5"].h, 16)
                scalar.wait_ge(S["gg"].h, 192)
                scalar.wait_ge(S["x1d"].h, 64)
                scalar.wait_ge(S["xl1d"].h, 64)
                for mh in range(2):
                    for e in range(NE):
                        i = NE * mh + e
                        for tq in range(2):
                            j = 2 * i + tq
                            if j >= 1:
                                scalar.wait_ge(S["wdm"].h, KD * j)
                            for fc in range(CI):
                                fi = CI * j + fc
                                scalar.wait_ge(S["hm"].h, fi + 1)
                                scalar.activation(h16[:, fc * TC:(fc + 1) * TC],
                                                  pM[fc % 2][:], AF.Gelu).then_inc(S["he"].h, 1)

            @block.vector
            def _(vector):
                vector.wait_ge(S["g4"].h, 16)
                vector.wait_ge(S["gg"].h, 192)
                vector.wait_ge(S["x1d"].h, 64)
                vector.wait_ge(S["xl1d"].h, 64)
                for mh in range(2):
                    for e in range(NE):
                        i = NE * mh + e
                        for tq in range(2):
                            j = 2 * i + tq
                            ej = 2 * (8 * mh + e) + tq
                            if e < 8 and not SKIP_CW:
                                vector.wait_ge(S["cwb"].h, ej + 1)
                                vector.tensor_copy(cwr16[:], pLa[:]).then_inc(S["cwe"].h, 1)
                                for fc in range(CI):
                                    vector.wait_ge(S["he"].h, CI * j + fc + 1)
                                    vector.tensor_mul(
                                        h16[:, fc * TC:(fc + 1) * TC],
                                        h16[:, fc * TC:(fc + 1) * TC],
                                        cwr16[:]).then_inc(S["hsc"].h, 1)
                            for dc in range(KD):
                                di = KD * j + dc
                                vector.wait_ge(S["wdm"].h, di + 1)
                                dst = macc[:, dc * 1024 + tq * TC:dc * 1024 + (tq + 1) * TC]
                                if e == 0:
                                    vector.tensor_copy(dst, pB[dc % 2][:]).then_inc(S["ma"].h, 1)
                                else:
                                    vector.tensor_add(dst, dst, pB[dc % 2][:]).then_inc(S["ma"].h, 1)
                    for dc in range(KD):
                        for tq in range(2):
                            i2 = 16 * mh + 2 * dc + tq
                            vector.wait_ge(S["x1ld"].h, 16 * (i2 + 1))
                            vector.tensor_add(
                                x1s[:, (i2 % 2) * TC:(i2 % 2 + 1) * TC],
                                x1s[:, (i2 % 2) * TC:(i2 % 2 + 1) * TC],
                                macc[:, dc * 1024 + tq * TC:dc * 1024 + (tq + 1) * TC]
                            ).then_inc(S["x2e"].h, 1)


def build_back0():
    nc = bass.Bass("TRN2", target_bir_lowering=False)
    io = dict(
        x=nc.dram_tensor("x", [DM, 2048], F32, kind="ExternalInput"),
        yf=nc.dram_tensor("yf", [DI, 2048], BF16, kind="ExternalInput"),
        w_out=nc.dram_tensor("w_out", [DI, DM], F32, kind="ExternalInput"),
        fg_w=nc.dram_tensor("fg_w", [DM, DM], F32, kind="ExternalInput"),
        fg_b=nc.dram_tensor("fg_b", [P, KD], F32, kind="ExternalInput"),
        wg=nc.dram_tensor("wg", [DM, 8], F32, kind="ExternalInput"),
        wu=nc.dram_tensor("wu", [9, DM, DI], F32, kind="ExternalInput"),
        wd=nc.dram_tensor("wd", [9, DI, DM], F32, kind="ExternalInput"),
        x2=nc.dram_tensor("x2", [DM, 2048], F32, kind="ExternalOutput"),
        x1_scr=nc.dram_tensor("x1_scr", [DM, 2048], F32,
                              kind="ExternalOutput" if DBG else "Internal"),
        xl1_scr=nc.dram_tensor("xl1_scr", [DM, 2048], BF16,
                               kind="ExternalOutput" if DBG else "Internal"),
    )
    with contextlib.ExitStack() as ctx:
        S = {n: Sem(ctx, nc, n) for n in BACK0_SEMS}
        block = ctx.enter_context(nc.Block())
        emit_back0(nc, block, S, io, "b0_")
    return nc



BACK1_SEMS = ("wldh wldg xld yld sq3 lb3 l31 l32 l33 l3b nfm x3v xl3 x3d xl3d "
              "x3ld mA mB eA eB h1v m2q e2q m3q e3q moq xov dbg1 dbg2 "
              "w1a0 w1a1 w1a2 w1a3 w2a0 w2a1 w2a2 w2a3 h1a0 h1a1 s3a0 s3a1 "
              "xlda0 xlda1 x4a0 x4a1").split()


def emit_back1(nc, block, S, io, pfx):
    x_in = io["x"]
    y_in = io["yf"]
    w_out = io["w_out"]
    w1t = io["w1t"]
    w2t = io["w2t"]
    h1t = io["h1t"]
    h2t = io["h2t"]
    w3t = io["w3t"]
    s1 = io["s1"]
    s2 = io["s2"]
    sh1 = io["sh1"]
    sh2 = io["sh2"]
    s3 = io["s3"]
    x4_out = io["x4"]
    x3_dram = io["x3_scr"]
    xl3_dram = io["xl3_scr"]
    h_dbg = io.get("h_dbg") or nc.dram_tensor(pfx + "h_dbg", [P, 3 * 32 * TC], BF16, kind="Internal")

    ones_c = nc.inline_tensor(np.ones((P, 1), np.float32), name=pfx + "ones_c")
    ones_r = nc.inline_tensor(np.ones((1, P), np.float32), name=pfx + "ones_r")
    cvals = nc.inline_tensor(np.tile(np.array([[1e-5, 1.0]], np.float32), (P, 1)), name=pfx + "cvals")
    NH = 32

    with contextlib.ExitStack() as ctx:
        E = ctx.enter_context
        rows = E(nc.sbuf_tensor(pfx + "rows", [1, 5 * TC], F32))
        sqcv = E(nc.sbuf_tensor(pfx + "sqcv", [P, 2 * TC], F32))
        gt = E(nc.sbuf_tensor(pfx + "gt", [P, TC], F32))
        soc = E(nc.sbuf_tensor(pfx + "soc", [P, 1], F32))
        sor = E(nc.sbuf_tensor(pfx + "sor", [1, P], F32))
        cst = E(nc.sbuf_tensor(pfx + "cst", [P, 2], F32))
        ss1 = E(nc.sbuf_tensor(pfx + "ss1", [P, 32], F32))
        ss2 = E(nc.sbuf_tensor(pfx + "ss2", [P, 32], F32))
        ssh1 = E(nc.sbuf_tensor(pfx + "ssh1", [P, 32], F32))
        ssh2 = E(nc.sbuf_tensor(pfx + "ssh2", [P, 32], F32))
        ss3 = E(nc.sbuf_tensor(pfx + "ss3", [P, KD], F32))
        pLa = E(nc.psum_tensor(pfx + "pLa", [P, TC], F32))
        pLb = E(nc.psum_tensor(pfx + "pLb", [P, TC], F32))
        pB0 = E(nc.psum_tensor(pfx + "pB0", [P, TC], F32))
        pB1 = E(nc.psum_tensor(pfx + "pB1", [P, TC], F32))
        pM0 = E(nc.psum_tensor(pfx + "pM0", [P, TC], F32))
        pM1 = E(nc.psum_tensor(pfx + "pM1", [P, TC], F32))
        pM = [pM0, pM1]
        pL = [pLa, pLb]
        pB = [pB0, pB1]
        with contextlib.ExitStack() as pctx:
            PE_ = pctx.enter_context
            xb = PE_(nc.sbuf_tensor(pfx + "xb", [P, 2 * KD * TC], F32))
            y16 = PE_(nc.sbuf_tensor(pfx + "y16", [P, 2 * CI * TC], BF16))
            wo16 = PE_(nc.sbuf_tensor(pfx + "wo16", [P, CI * DM], BF16))
            x3c = PE_(nc.sbuf_tensor(pfx + "x3c", [P, KD * TC], F32))
            xl316 = PE_(nc.sbuf_tensor(pfx + "xl316", [P, KD * TC], BF16))

            @block.sync
            def _(sync):
                def ld(dst, src):
                    sync.dma_start(out=dst, in_=src).then_inc(S["wldh"].h, 16)
                ld(soc[:], ones_c.ap())
                ld(sor[:], ones_r.ap())
                ld(cst[:], cvals.ap())
                ld(ss1[:], s1.ap())
                ld(ss2[:], s2.ap())
                ld(ssh1[:], sh1.ap())
                ld(ssh2[:], sh2.ap())
                ld(ss3[:], s3.ap())
                x3 = x_in.ap().rearrange("(kc p) t -> p kc t", p=P)
                y3 = y_in.ap().rearrange("(kc p) t -> p kc t", p=P)
                for tc in range(min(2, NCO)):
                    dx = xb[:, (tc % 2) * KD * TC:(tc % 2 + 1) * KD * TC].rearrange(
                        "p (kc t) -> p kc t", kc=KD)
                    sync.dma_start(out=dx, in_=x3[:, :, tc * TC:(tc + 1) * TC]).then_inc(S["xld"].h, 16)
                    dy = y16[:, (tc % 2) * CI * TC:(tc % 2 + 1) * CI * TC].rearrange(
                        "p (kc t) -> p kc t", kc=CI)
                    sync.dma_start(out=dy, in_=y3[:, :, tc * TC:(tc + 1) * TC]).then_inc(S["yld"].h, 16)
                for tc in range(NCO):
                    if tc + 2 < NCO:
                        sync.wait_ge(S["x3v"].h, KD * (tc + 1))
                        dx = xb[:, (tc % 2) * KD * TC:(tc % 2 + 1) * KD * TC].rearrange(
                            "p (kc t) -> p kc t", kc=KD)
                        sync.dma_start(out=dx,
                                       in_=x3[:, :, (tc + 2) * TC:(tc + 3) * TC]).then_inc(S["xld"].h, 16)
                        sync.wait_ge(S["nfm"].h, KD * (tc + 1))
                        dy = y16[:, (tc % 2) * CI * TC:(tc % 2 + 1) * CI * TC].rearrange(
                            "p (kc t) -> p kc t", kc=CI)
                        sync.dma_start(out=dy,
                                       in_=y3[:, :, (tc + 2) * TC:(tc + 3) * TC]).then_inc(S["yld"].h, 16)
                    sync.wait_ge(S["x3v"].h, KD * (tc + 1))
                    sync.dma_start(
                        out=x3_dram.ap().rearrange("(kc p) t -> p kc t", p=P)[:, :, tc * TC:(tc + 1) * TC],
                        in_=x3c[:].rearrange("p (kc t) -> p kc t", kc=KD)).then_inc(S["x3d"].h, 16)
                    sync.wait_ge(S["xl3"].h, KD * (tc + 1))
                    sync.dma_start(
                        out=xl3_dram.ap().rearrange("(kc p) t -> p kc t", p=P)[:, :, tc * TC:(tc + 1) * TC],
                        in_=xl316[:].rearrange("p (kc t) -> p kc t", kc=KD)).then_inc(S["xl3d"].h, 16)

            @block.gpsimd
            def _(g):
                g.dma_start(out=wo16[:].rearrange("p (kc m) -> p kc m", kc=CI),
                            in_=w_out.ap().rearrange("(kc p) m -> p kc m", p=P)).then_inc(S["wldg"].h, 16)

            @block.scalar
            def _(scalar):
                scalar.wait_ge(S["wldh"].h, 16 * 8)
                scalar.wait_ge(S["wldg"].h, 16)
                for tc in range(NCO):
                    for kc in range(KD):
                        scalar.wait_ge(S["x3v"].h, KD * tc + kc + 1)
                        if KD * tc + kc >= 2:
                            scalar.wait_ge(S["lb3"].h, KD * tc + kc - 1)
                        scalar.activation(sqcv[:, (kc % 2) * TC:(kc % 2 + 1) * TC],
                                          x3c[:, kc * TC:(kc + 1) * TC],
                                          AF.Square).then_inc(S["sq3"].h, 1)
                    scalar.wait_ge(S["l31"].h, tc + 1)
                    scalar.activation(rows[:, 3 * TC:4 * TC], rows[:, 2 * TC:3 * TC],
                                      AF.Ln, bias=cst[0:1, 0:1])
                    scalar.activation(rows[:, 3 * TC:4 * TC], rows[:, 3 * TC:4 * TC],
                                      AF.Exp, scale=-0.5).then_inc(S["l32"].h, 1)

            @block.tensor
            def _(tensor):
                tensor.wait_ge(S["wldh"].h, 16 * 8)
                tensor.wait_ge(S["wldg"].h, 16)
                for tc in range(NCO):
                    yoff = (tc % 2) * CI * TC
                    tensor.wait_ge(S["yld"].h, 16 * (tc + 1))
                    for m in range(KD):
                        if KD * tc + m >= 2:
                            tensor.wait_ge(S["x3v"].h, KD * tc + m - 1)
                        for kc in range(CI):
                            ins = tensor.matmul(
                                pM[m % 2][:], wo16[:, kc * DM + m * P:kc * DM + (m + 1) * P],
                                y16[:, yoff + kc * TC:yoff + (kc + 1) * TC],
                                start=(kc == 0), stop=(kc == CI - 1))
                        ins.then_inc(S["nfm"].h, 1)
                    for kc in range(KD):
                        tensor.wait_ge(S["x3v"].h, KD * tc + kc + 1)
                        tensor.matmul(pLa[:1, :], soc[:], x3c[:, kc * TC:(kc + 1) * TC],
                                      start=(kc == 0), stop=(kc == KD - 1))
                    for kc in range(KD):
                        tensor.wait_ge(S["sq3"].h, KD * tc + kc + 1)
                        tensor.matmul(pLb[:1, :], soc[:],
                                      sqcv[:, (kc % 2) * TC:(kc % 2 + 1) * TC],
                                      start=(kc == 0),
                                      stop=(kc == KD - 1)).then_inc(S["lb3"].h, 1)
                    tensor.wait_ge(S["l33"].h, tc + 1)
                    tensor.wait_ge(S["xl3"].h, KD * tc)
                    tensor.matmul(pB0[:], sor[:], rows[:, 3 * TC:4 * TC], start=True, stop=True)
                    tensor.matmul(pB1[:], sor[:], rows[:, 4 * TC:5 * TC],
                                  start=True, stop=True).then_inc(S["l3b"].h, 1)

            @block.vector
            def _(vector):
                vector.wait_ge(S["wldh"].h, 16 * 8)
                for tc in range(NCO):
                    xoff = (tc % 2) * KD * TC
                    if tc >= 1:
                        vector.wait_ge(S["x3d"].h, 16 * tc)
                    vector.wait_ge(S["xld"].h, 16 * (tc + 1))
                    for m in range(KD):
                        vector.wait_ge(S["nfm"].h, KD * tc + m + 1)
                        vector.tensor_add(x3c[:, m * TC:(m + 1) * TC],
                                          xb[:, xoff + m * TC:xoff + (m + 1) * TC],
                                          pM[m % 2][:]).then_inc(S["x3v"].h, 1)
                    vector.wait_ge(S["lb3"].h, KD * (tc + 1))
                    vector.tensor_scalar(rows[:, 0:TC], pLa[:1, :], 1.0 / DM, None, op0=ALU.mult)
                    vector.tensor_scalar(rows[:, TC:2 * TC], pLb[:1, :], 1.0 / DM, None, op0=ALU.mult)
                    vector.tensor_mul(gt[0:1, :], rows[:, 0:TC], rows[:, 0:TC])
                    vector.tensor_sub(rows[:, 2 * TC:3 * TC], rows[:, TC:2 * TC],
                                      gt[0:1, :]).then_inc(S["l31"].h, 1)
                    vector.wait_ge(S["l32"].h, tc + 1)
                    vector.tensor_mul(rows[:, 4 * TC:5 * TC], rows[:, 0:TC],
                                      rows[:, 3 * TC:4 * TC]).then_inc(S["l33"].h, 1)
                    vector.wait_ge(S["l3b"].h, tc + 1)
                    if tc >= 1:
                        vector.wait_ge(S["xl3d"].h, 16 * tc)
                    for kc in range(KD):
                        vector.tensor_mul(gt[:], x3c[:, kc * TC:(kc + 1) * TC], pB0[:])
                        vector.tensor_sub(xl316[:, kc * TC:(kc + 1) * TC], gt[:],
                                          pB1[:]).then_inc(S["xl3"].h, 1)

        with contextlib.ExitStack() as pctx:
            PE_ = pctx.enter_context
            xl3b = PE_(nc.sbuf_tensor(pfx + "xl3b", [P, 2 * KD * TC], BF16, unsafe_allow_realloc=True))
            x3s = PE_(nc.sbuf_tensor(pfx + "x3s", [P, KD * TC], F32, unsafe_allow_realloc=True))
            hA = PE_(nc.sbuf_tensor(pfx + "hA", [P, NH * TC], BF16, unsafe_allow_realloc=True))
            hB = PE_(nc.sbuf_tensor(pfx + "hB", [P, NH * TC], BF16, unsafe_allow_realloc=True))
            w1s = PE_(nc.sbuf_tensor(pfx + "w1s", [P, 4 * KD * P], BF16, unsafe_allow_realloc=True))
            w2s = PE_(nc.sbuf_tensor(pfx + "w2s", [P, 4 * KD * P], BF16, unsafe_allow_realloc=True))
            hws = PE_(nc.sbuf_tensor(pfx + "hws", [P, 2 * NH * P], BF16, unsafe_allow_realloc=True))
            w3s = PE_(nc.sbuf_tensor(pfx + "w3s", [P, 2 * NH * P], BF16, unsafe_allow_realloc=True))
            ta = PE_(nc.sbuf_tensor(pfx + "ta", [P, 2 * TC], F32, unsafe_allow_realloc=True))
            tb = PE_(nc.sbuf_tensor(pfx + "tb", [P, 2 * TC], F32, unsafe_allow_realloc=True))
            x4b = PE_(nc.sbuf_tensor(pfx + "x4b", [P, 2 * TC], F32, unsafe_allow_realloc=True))

            x33 = x3_dram.ap().rearrange("(kc p) t -> p kc t", p=P)
            xl33 = xl3_dram.ap().rearrange("(kc p) t -> p kc t", p=P)
            x43 = x4_out.ap().rearrange("(kc p) t -> p kc t", p=P)
            w1r = w1t.ap().rearrange("(kc p) m -> p kc m", p=P)
            w2r = w2t.ap().rearrange("(kc p) m -> p kc m", p=P)
            h1r = h1t.ap().rearrange("(kc p) m -> p kc m", p=P)
            h2r = h2t.ap().rearrange("(kc p) m -> p kc m", p=P)
            w3r = w3t.ap().rearrange("(kc p) m -> p kc m", p=P)
            NKH = 4096 // P

            @block.sync
            def _(sync):
                sync.wait_ge(S["x3v"].h, KD * NCO)
                sync.wait_ge(S["xl3"].h, KD * NCO)
                sync.wait_ge(S["l3b"].h, NCO)
                sync.wait_ge(S["l32"].h, NCO)
                sync.wait_ge(S["x3d"].h, 16 * NCO)
                sync.wait_ge(S["xl3d"].h, 16 * NCO)
                sync.wait_ge(S["xld"].h, 16 * NCO)
                sync.wait_ge(S["yld"].h, 16 * NCO)
                sync.wait_ge(S["nfm"].h, KD * NCO)
                sync.wait_ge(S["sq3"].h, KD * NCO)
                for tc in range(min(2, NCO)):
                    sync.wait_ge(S["xl3d"].h, 64)
                    sync.dma_start(
                        out=xl3b[:, (tc % 2) * KD * TC:(tc % 2 + 1) * KD * TC].rearrange(
                            "p (kc t) -> p kc t", kc=KD),
                        in_=xl33[:, :, tc * TC:(tc + 1) * TC]).then_inc(S["xlda%d" % (tc % 2)].h, 16)
                for tc in range(NCO):
                    if tc + 2 < NCO:
                        sync.wait_ge(S["mB"].h, 32 * (tc + 1))
                        sync.dma_start(
                            out=xl3b[:, (tc % 2) * KD * TC:(tc % 2 + 1) * KD * TC].rearrange(
                                "p (kc t) -> p kc t", kc=KD),
                            in_=xl33[:, :, (tc + 2) * TC:(tc + 3) * TC]).then_inc(S["xlda%d" % (tc % 2)].h, 16)
                    sync.wait_ge(S["x3d"].h, 64)
                    if tc >= 1:
                        sync.wait_ge(S["xov"].h, KD * tc)
                    sync.dma_start(
                        out=x3s[:].rearrange("p (kc t) -> p kc t", kc=KD),
                        in_=x33[:, :, tc * TC:(tc + 1) * TC]).then_inc(S["x3ld"].h, 16)
                    for dc in range(KD):
                        i = KD * tc + dc
                        sync.wait_ge(S["xov"].h, i + 1)
                        sync.dma_start(out=x43[:, dc, tc * TC:(tc + 1) * TC],
                                       in_=x4b[:, (i % 2) * TC:(i % 2 + 1) * TC]).then_inc(S["x4a%d" % (i % 2)].h, 16)
                sync.wait_ge(S["x4a0"].h, 16 * KD * NCO // 2)
                sync.wait_ge(S["x4a1"].h, 16 * KD * NCO // 2)

            @block.gpsimd
            def _(g):
                g.wait_ge(S["x3v"].h, KD * NCO)
                g.wait_ge(S["xl3"].h, KD * NCO)
                g.wait_ge(S["l3b"].h, NCO)
                g.wait_ge(S["l32"].h, NCO)
                g.wait_ge(S["x3d"].h, 16 * NCO)
                g.wait_ge(S["xl3d"].h, 16 * NCO)
                g.wait_ge(S["xld"].h, 16 * NCO)
                g.wait_ge(S["yld"].h, 16 * NCO)
                g.wait_ge(S["nfm"].h, KD * NCO)
                g.wait_ge(S["sq3"].h, KD * NCO)
                for tc in range(NCO):
                    for oc in range(NH):
                        i = NH * tc + oc
                        if i >= 4:
                            g.wait_ge(S["mA"].h, i - 3)
                        g.dma_start(out=w1s[:, (oc % 4) * KD * P:(oc % 4 + 1) * KD * P]
                                    .rearrange("p (kc m) -> p kc m", kc=KD),
                                    in_=w1r[:, :, oc * P:(oc + 1) * P]).then_inc(S["w1a%d" % (oc % 4)].h, 16)
                        if i >= 4:
                            g.wait_ge(S["mB"].h, i - 3)
                        g.dma_start(out=w2s[:, (oc % 4) * KD * P:(oc % 4 + 1) * KD * P]
                                    .rearrange("p (kc m) -> p kc m", kc=KD),
                                    in_=w2r[:, :, oc * P:(oc + 1) * P]).then_inc(S["w2a%d" % (oc % 4)].h, 16)
                    for oc in range(NH):
                        i = NH * tc + oc
                        if i >= 2:
                            g.wait_ge(S["m2q"].h, i - 1)
                        g.dma_start(out=hws[:, (oc % 2) * NKH * P:(oc % 2 + 1) * NKH * P]
                                    .rearrange("p (kc m) -> p kc m", kc=NKH),
                                    in_=h1r[:, :, oc * P:(oc + 1) * P]).then_inc(S["h1a%d" % (oc % 2)].h, 16)
                    for oc in range(NH):
                        i = NH * tc + oc
                        if oc < 2:
                            g.wait_ge(S["moq"].h, KD * tc)
                        if i >= 2:
                            g.wait_ge(S["m3q"].h, i - 1)
                        g.dma_start(out=w3s[:, (oc % 2) * NKH * P:(oc % 2 + 1) * NKH * P]
                                    .rearrange("p (kc m) -> p kc m", kc=NKH),
                                    in_=h2r[:, :, oc * P:(oc + 1) * P]).then_inc(S["s3a%d" % (oc % 2)].h, 16)
                    for dc in range(KD):
                        i = KD * tc + dc
                        if dc < 2:
                            g.wait_ge(S["m3q"].h, NH * (tc + 1))
                        if i >= 2:
                            g.wait_ge(S["moq"].h, i - 1)
                        g.dma_start(out=w3s[:, (dc % 2) * NKH * P:(dc % 2 + 1) * NKH * P]
                                    .rearrange("p (kc m) -> p kc m", kc=NKH),
                                    in_=w3r[:, :, dc * P:(dc + 1) * P]).then_inc(S["s3a%d" % (dc % 2)].h, 16)

            @block.tensor
            def _(tensor):
                tensor.wait_ge(S["x3v"].h, KD * NCO)
                tensor.wait_ge(S["xl3"].h, KD * NCO)
                tensor.wait_ge(S["l3b"].h, NCO)
                tensor.wait_ge(S["l32"].h, NCO)
                tensor.wait_ge(S["x3d"].h, 16 * NCO)
                tensor.wait_ge(S["xl3d"].h, 16 * NCO)
                tensor.wait_ge(S["xld"].h, 16 * NCO)
                tensor.wait_ge(S["yld"].h, 16 * NCO)
                tensor.wait_ge(S["nfm"].h, KD * NCO)
                tensor.wait_ge(S["sq3"].h, KD * NCO)
                for tc in range(NCO):
                    xoff = (tc % 2) * KD * TC
                    tensor.wait_ge(S["xlda%d" % (tc % 2)].h, 16 * (tc // 2 + 1))
                    for oc in range(NH):
                        i = NH * tc + oc
                        tensor.wait_ge(S["w1a%d" % (oc % 4)].h, 16 * (tc * 8 + oc // 4 + 1))
                        if oc == 0:
                            tensor.wait_ge(S["xov"].h, KD * tc)
                        if i >= 1:
                            tensor.wait_ge(S["eA"].h, i)
                        for kc in range(KD):
                            ins = tensor.matmul(
                                pM0[:], w1s[:, (oc % 4) * KD * P + kc * P:(oc % 4) * KD * P + (kc + 1) * P],
                                xl3b[:, xoff + kc * TC:xoff + (kc + 1) * TC],
                                start=(kc == 0), stop=(kc == KD - 1))
                        ins.then_inc(S["mA"].h, 1)
                        tensor.wait_ge(S["w2a%d" % (oc % 4)].h, 16 * (tc * 8 + oc // 4 + 1))
                        if i >= 1:
                            tensor.wait_ge(S["eB"].h, i)
                        for kc in range(KD):
                            ins = tensor.matmul(
                                pM1[:], w2s[:, (oc % 4) * KD * P + kc * P:(oc % 4) * KD * P + (kc + 1) * P],
                                xl3b[:, xoff + kc * TC:xoff + (kc + 1) * TC],
                                start=(kc == 0), stop=(kc == KD - 1))
                        ins.then_inc(S["mB"].h, 1)
                    for oc in range(NH):
                        i = NH * tc + oc
                        tensor.wait_ge(S["h1a%d" % (oc % 2)].h, 16 * (tc * 16 + oc // 2 + 1))
                        tensor.wait_ge(S["h1v"].h, NH * tc + NH)
                        if i >= 2:
                            tensor.wait_ge(S["e2q"].h, i - 1)
                        for kc in range(NKH):
                            ins = tensor.matmul(
                                pB[oc % 2][:], hws[:, (oc % 2) * NKH * P + kc * P:(oc % 2) * NKH * P + (kc + 1) * P],
                                hA[:, kc * TC:(kc + 1) * TC],
                                start=(kc == 0), stop=(kc == NKH - 1))
                        ins.then_inc(S["m2q"].h, 1)
                    for oc in range(NH):
                        i = NH * tc + oc
                        tensor.wait_ge(S["s3a%d" % (oc % 2)].h, 16 * (tc * 20 + oc // 2 + 1))
                        tensor.wait_ge(S["e2q"].h, NH * tc + NH)
                        tensor.wait_ge(S["h1v"].h, NH * (tc + 1))
                        if i >= 2:
                            tensor.wait_ge(S["e3q"].h, i - 1)
                        for kc in range(NKH):
                            ins = tensor.matmul(
                                pL[oc % 2][:], w3s[:, (oc % 2) * NKH * P + kc * P:(oc % 2) * NKH * P + (kc + 1) * P],
                                hB[:, kc * TC:(kc + 1) * TC],
                                start=(kc == 0), stop=(kc == NKH - 1))
                        ins.then_inc(S["m3q"].h, 1)
                    for dc in range(KD):
                        i = KD * tc + dc
                        tensor.wait_ge(S["s3a%d" % (dc % 2)].h, 16 * (tc * 20 + 16 + dc // 2 + 1))
                        tensor.wait_ge(S["e3q"].h, NH * (tc + 1))
                        tensor.wait_ge(S["eA"].h, NH * (tc + 1))
                        tensor.wait_ge(S["eB"].h, NH * (tc + 1))
                        if i >= 2:
                            tensor.wait_ge(S["xov"].h, i - 1)
                        for kc in range(NKH):
                            ins = tensor.matmul(
                                pM[dc % 2][:], w3s[:, (dc % 2) * NKH * P + kc * P:(dc % 2) * NKH * P + (kc + 1) * P],
                                hA[:, kc * TC:(kc + 1) * TC],
                                start=(kc == 0), stop=(kc == NKH - 1))
                        ins.then_inc(S["moq"].h, 1)

            @block.scalar
            def _(scalar):
                scalar.wait_ge(S["x3v"].h, KD * NCO)
                scalar.wait_ge(S["xl3"].h, KD * NCO)
                scalar.wait_ge(S["l3b"].h, NCO)
                scalar.wait_ge(S["l32"].h, NCO)
                scalar.wait_ge(S["x3d"].h, 16 * NCO)
                scalar.wait_ge(S["xl3d"].h, 16 * NCO)
                scalar.wait_ge(S["xld"].h, 16 * NCO)
                scalar.wait_ge(S["yld"].h, 16 * NCO)
                scalar.wait_ge(S["nfm"].h, KD * NCO)
                scalar.wait_ge(S["sq3"].h, KD * NCO)
                for tc in range(NCO):
                    if DBG and tc == 1:
                        scalar.wait_ge(S["xlld"].h, 32)
                        scalar.dma_start(out=h_dbg.ap()[:, 0:KD * TC],
                                         in_=xl3b[:, KD * TC:2 * KD * TC]).then_inc(S["dbg1"].h, 16)
                    for oc in range(NH):
                        i = NH * tc + oc
                        scalar.wait_ge(S["mA"].h, i + 1)
                        if i >= 2:
                            scalar.wait_ge(S["h1v"].h, i - 1)
                        scalar.activation(ta[:, (i % 2) * TC:(i % 2 + 1) * TC], pM0[:],
                                          AF.Silu, scale=ss1[:, oc:oc + 1]).then_inc(S["eA"].h, 1)
                        scalar.wait_ge(S["mB"].h, i + 1)
                        scalar.mul(tb[:, (i % 2) * TC:(i % 2 + 1) * TC], pM1[:],
                                   ss2[:, oc:oc + 1]).then_inc(S["eB"].h, 1)
                    if DBG and tc == 1:
                        scalar.wait_ge(S["h1v"].h, 64)
                        scalar.dma_start(out=h_dbg.ap()[:, 32 * TC:64 * TC],
                                         in_=hA[:]).then_inc(S["dbg2"].h, 16)
                    scalar.wait_ge(S["m3q"].h, NH * tc)
                    for oc in range(NH):
                        i = NH * tc + oc
                        scalar.wait_ge(S["m2q"].h, i + 1)
                        scalar.activation(hB[:, oc * TC:(oc + 1) * TC], pB[oc % 2][:],
                                          AF.Gelu, scale=ssh1[:, oc:oc + 1]).then_inc(S["e2q"].h, 1)
                    for oc in range(NH):
                        i = NH * tc + oc
                        scalar.wait_ge(S["m3q"].h, i + 1)
                        scalar.activation(hA[:, oc * TC:(oc + 1) * TC], pL[oc % 2][:],
                                          AF.Gelu, scale=ssh2[:, oc:oc + 1]).then_inc(S["e3q"].h, 1)

            @block.vector
            def _(vector):
                vector.wait_ge(S["x3v"].h, KD * NCO)
                vector.wait_ge(S["xl3"].h, KD * NCO)
                vector.wait_ge(S["l3b"].h, NCO)
                vector.wait_ge(S["l32"].h, NCO)
                vector.wait_ge(S["x3d"].h, 16 * NCO)
                vector.wait_ge(S["xl3d"].h, 16 * NCO)
                vector.wait_ge(S["xld"].h, 16 * NCO)
                vector.wait_ge(S["yld"].h, 16 * NCO)
                vector.wait_ge(S["nfm"].h, KD * NCO)
                vector.wait_ge(S["sq3"].h, KD * NCO)
                for tc in range(NCO):
                    vector.wait_ge(S["moq"].h, KD * tc)
                    for oc in range(NH):
                        i = NH * tc + oc
                        vector.wait_ge(S["eA"].h, i + 1)
                        vector.wait_ge(S["eB"].h, i + 1)
                        vector.tensor_mul(hA[:, oc * TC:(oc + 1) * TC],
                                          ta[:, (i % 2) * TC:(i % 2 + 1) * TC],
                                          tb[:, (i % 2) * TC:(i % 2 + 1) * TC]).then_inc(S["h1v"].h, 1)
                    vector.wait_ge(S["x3ld"].h, 16 * (tc + 1))
                    for dc in range(KD):
                        i = KD * tc + dc
                        vector.wait_ge(S["moq"].h, i + 1)
                        if i >= 2:
                            vector.wait_ge(S["x4a%d" % (i % 2)].h, 16 * ((i - 2) // 2 + 1))
                        vector.scalar_tensor_tensor(
                            x4b[:, (i % 2) * TC:(i % 2 + 1) * TC], pM[dc % 2][:],
                            ss3[:, dc:dc + 1], x3s[:, dc * TC:(dc + 1) * TC],
                            op0=ALU.mult, op1=ALU.add).then_inc(S["xov"].h, 1)


def build_back1():
    nc = bass.Bass("TRN2", target_bir_lowering=False)
    io = dict(
        x=nc.dram_tensor("x", [DM, 2048], F32, kind="ExternalInput"),
        yf=nc.dram_tensor("yf", [DI, 2048], BF16, kind="ExternalInput"),
        w_out=nc.dram_tensor("w_out", [DI, DM], F32, kind="ExternalInput"),
        w1t=nc.dram_tensor("w1t", [DM, 4096], F32, kind="ExternalInput"),
        w2t=nc.dram_tensor("w2t", [DM, 4096], F32, kind="ExternalInput"),
        h1t=nc.dram_tensor("h1t", [4096, 4096], F32, kind="ExternalInput"),
        h2t=nc.dram_tensor("h2t", [4096, 4096], F32, kind="ExternalInput"),
        w3t=nc.dram_tensor("w3t", [4096, DM], F32, kind="ExternalInput"),
        s1=nc.dram_tensor("s1", [P, 32], F32, kind="ExternalInput"),
        s2=nc.dram_tensor("s2", [P, 32], F32, kind="ExternalInput"),
        sh1=nc.dram_tensor("sh1", [P, 32], F32, kind="ExternalInput"),
        sh2=nc.dram_tensor("sh2", [P, 32], F32, kind="ExternalInput"),
        s3=nc.dram_tensor("s3", [P, KD], F32, kind="ExternalInput"),
        x4=nc.dram_tensor("x4", [DM, 2048], F32, kind="ExternalOutput"),
        x3_scr=nc.dram_tensor("x3_scr", [DM, 2048], F32,
                              kind="ExternalOutput" if DBG else "Internal"),
        xl3_scr=nc.dram_tensor("xl3_scr", [DM, 2048], BF16,
                               kind="ExternalOutput" if DBG else "Internal"),
        h_dbg=nc.dram_tensor("h_dbg", [P, 3 * 32 * TC], BF16,
                             kind="ExternalOutput" if DBG else "Internal"),
    )
    with contextlib.ExitStack() as ctx:
        S = {n: Sem(ctx, nc, n) for n in BACK1_SEMS}
        block = ctx.enter_context(nc.Block())
        emit_back1(nc, block, S, io, "b1_")
    return nc



I32 = mybir.dt.int32
DBGF = False


def build_fused():
    nc = bass.Bass("TRN2", target_bir_lowering=False)
    KI = "ExternalOutput" if DBGF else "Internal"
    role_t = nc.dram_tensor("role", [1, 1], I32, kind="ExternalInput")
    runid_t = nc.dram_tensor("runid", [1, 1], I32, kind="ExternalInput")

    ysh = nc.dram_tensor("ysh", [2, DH, 2048], BF16, kind="Internal", addr_space="Shared")
    xsh = nc.dram_tensor("xsh", [2, DM, 2048], F32, kind="Internal", addr_space="Shared")
    flags = nc.dram_tensor("flags", [2, 4], I32, kind="Internal", addr_space="Shared")

    yg0 = nc.dram_tensor("f0_yg", [DH, 4096], BF16, kind=KI)
    y0loc = nc.dram_tensor("y0loc", [DI, 2048], BF16, kind=KI)
    x2own = nc.dram_tensor("x2own", [DM, 2048], F32, kind=KI)
    x2pair = nc.dram_tensor("x2pair", [DM, 4096], F32, kind=KI)
    yg1 = nc.dram_tensor("f1_yg", [DH, 4096], BF16, kind=KI)
    y1loc = nc.dram_tensor("y1loc", [DI, 2048], BF16, kind=KI)

    def front_io(pfx):
        return dict(
            w_in=nc.dram_tensor(pfx + "w_in", [DM, 3072], F32, kind="ExternalInput"),
            w_cv=nc.dram_tensor(pfx + "w_cv", [P, CI * 4], F32, kind="ExternalInput"),
            b_cv=nc.dram_tensor(pfx + "b_cv", [P, CI], F32, kind="ExternalInput"),
            w_x=nc.dram_tensor(pfx + "w_x", [DI, 96], F32, kind="ExternalInput"),
            w_dt=nc.dram_tensor(pfx + "w_dt", [64, DH], F32, kind="ExternalInput"),
            b_dt=nc.dram_tensor(pfx + "b_dt", [P, DHC], F32, kind="ExternalInput"),
            a_log=nc.dram_tensor(pfx + "a_log", [P, DHC * NST], F32, kind="ExternalInput"),
            d_vec=nc.dram_tensor(pfx + "d_vec", [P, DHC], F32, kind="ExternalInput"),
        )

    iof0 = front_io("f0_")
    iof0["x"] = nc.dram_tensor("f0_x", [DM, 4096], F32, kind="ExternalInput")
    iof0["yg"] = yg0
    iof1 = front_io("f1_")
    iof1["x"] = x2pair
    iof1["yg"] = yg1
    iob0 = dict(
        x=nc.dram_tensor("b0_x", [DM, 2048], F32, kind="ExternalInput"),
        yf=y0loc,
        w_out=nc.dram_tensor("b0_w_out", [DI, DM], F32, kind="ExternalInput"),
        fg_w=nc.dram_tensor("b0_fg_w", [DM, DM], F32, kind="ExternalInput"),
        fg_b=nc.dram_tensor("b0_fg_b", [P, KD], F32, kind="ExternalInput"),
        wg=nc.dram_tensor("b0_wg", [DM, 8], F32, kind="ExternalInput"),
        wu=nc.dram_tensor("b0_wu", [9, DM, DI], F32, kind="ExternalInput"),
        wd=nc.dram_tensor("b0_wd", [9, DI, DM], F32, kind="ExternalInput"),
        x2=x2own,
        x1_scr=nc.dram_tensor("b0_x1_scr", [DM, 2048], F32, kind=KI),
        xl1_scr=nc.dram_tensor("b0_xl1_scr", [DM, 2048], BF16, kind=KI),
    )
    iob1 = dict(
        x=x2own,
        yf=y1loc,
        w_out=nc.dram_tensor("b1_w_out", [DI, DM], F32, kind="ExternalInput"),
        w1t=nc.dram_tensor("b1_w1t", [DM, 4096], F32, kind="ExternalInput"),
        w2t=nc.dram_tensor("b1_w2t", [DM, 4096], F32, kind="ExternalInput"),
        h1t=nc.dram_tensor("b1_h1t", [4096, 4096], F32, kind="ExternalInput"),
        h2t=nc.dram_tensor("b1_h2t", [4096, 4096], F32, kind="ExternalInput"),
        w3t=nc.dram_tensor("b1_w3t", [4096, DM], F32, kind="ExternalInput"),
        s1=nc.dram_tensor("b1_s1", [P, 32], F32, kind="ExternalInput"),
        s2=nc.dram_tensor("b1_s2", [P, 32], F32, kind="ExternalInput"),
        sh1=nc.dram_tensor("b1_sh1", [P, 32], F32, kind="ExternalInput"),
        sh2=nc.dram_tensor("b1_sh2", [P, 32], F32, kind="ExternalInput"),
        s3=nc.dram_tensor("b1_s3", [P, KD], F32, kind="ExternalInput"),
        x4=nc.dram_tensor("x4", [DM, 2048], F32, kind="ExternalOutput"),
        x3_scr=nc.dram_tensor("b1_x3_scr", [DM, 2048], F32, kind=KI),
        xl3_scr=nc.dram_tensor("b1_xl3_scr", [DM, 2048], BF16, kind=KI),
    )

    def run_phase(sems, pfx, fn, last=False):
        boundary = contextlib.nullcontext() if last else nc.reset_on_exit()
        with boundary:
            with contextlib.ExitStack() as ctx:
                S = {n: Sem(ctx, nc, pfx + n) for n in sems}
                block = ctx.enter_context(nc.Block())
                fn(S, block)

    def run_exchange(k, aps_for_role):
        with nc.reset_on_exit():
            with contextlib.ExitStack() as ctx:
                gs = ctx.enter_context(nc.semaphore("ex%d_s" % k))
                go = ctx.enter_context(nc.semaphore("ex%d_o" % k))
                block = ctx.enter_context(nc.Block())

                @block.gpsimd
                def _(g):
                    with g.register("xr%d" % k) as rr, g.register("xf%d" % k) as rf, \
                         g.register("xc%d" % k) as cond, g.register("xi%d" % k) as rid, \
                         g.register("xt%d" % k) as tgt:
                        g.reg_load(rr, role_t.ap()[0:1, 0:1])
                        g.reg_load(rid, runid_t.ap()[0:1, 0:1])
                        g.reg_alu(tgt, rid, k + 1, ALU.add)
                        for r in (0, 1):
                            br = g.If_eq(rr, 0) if r == 0 else g.Else()
                            with br:
                                send_src, sh_send, own_src, own_dst, sh_recv, recv_dst = aps_for_role(r)
                                g.dma_start(out=sh_send, in_=send_src).then_inc(gs, 16)
                                g.dma_start(out=own_dst, in_=own_src).then_inc(go, 16)
                                g.wait_ge(gs, 16)
                                g.reg_save(flags.ap()[r:r + 1, k:k + 1], tgt)
                                g.reg_mov(cond, 1)
                                with g.While(cond):
                                    g.reg_load(rf, flags.ap()[1 - r:2 - r, k:k + 1])
                                    g.reg_alu(cond, rf, tgt, ALU.not_equal)
                                g.dma_start(out=recv_dst, in_=sh_recv).then_inc(go, 16)
                                g.wait_ge(go, 32)

    def yg_exchange_aps(yg, yloc):
        def aps(r):
            po, pp = r * 2048, (1 - r) * 2048
            return (yg.ap()[:, pp:pp + 2048], ysh.ap()[r],
                    yg.ap()[:, po:po + 2048], yloc.ap()[0:DH, :],
                    ysh.ap()[1 - r], yloc.ap()[DH:DI, :])
        return aps

    def x2_exchange_aps(r):
        po, pp = r * 2048, (1 - r) * 2048
        return (x2own.ap(), xsh.ap()[r],
                x2own.ap(), x2pair.ap()[:, po:po + 2048],
                xsh.ap()[1 - r], x2pair.ap()[:, pp:pp + 2048])

    run_phase(FRONT_SEMS, "f0_", lambda S, b: emit_front(nc, b, S, iof0, 0, "f0_"))
    run_exchange(0, yg_exchange_aps(yg0, y0loc))
    run_phase(BACK0_SEMS, "b0_", lambda S, b: emit_back0(nc, b, S, iob0, "b0_"))
    run_exchange(1, x2_exchange_aps)
    run_phase(FRONT_SEMS, "f1_", lambda S, b: emit_front(nc, b, S, iof1, 1, "f1_"))
    run_exchange(2, yg_exchange_aps(yg1, y1loc))
    run_phase(BACK1_SEMS, "b1_", lambda S, b: emit_back1(nc, b, S, iob1, "b1_"), last=True)
    return nc



def _back0_inputs(ins, h, yg_pair):
    tok = slice(h * 2048, (h + 1) * 2048)
    g = lambda k: np.asarray(ins["L0_" + k])
    wu = np.concatenate([g("ffn_Wu"), g("ffn_W1")[None]], 0)
    wd = np.concatenate([g("ffn_Wd"), g("ffn_W2")[None]], 0)
    d = dict(
        w_out=np.ascontiguousarray(g("attn_out_proj_w"), np.float32),
        fg_w=np.ascontiguousarray(g("attn_fg_w"), np.float32),
        fg_b=np.ascontiguousarray(g("attn_fg_b").reshape(KD, P).T, np.float32),
        wg=np.ascontiguousarray(g("ffn_Wg"), np.float32),
        wu=np.ascontiguousarray(wu, np.float32),
        wd=np.ascontiguousarray(wd, np.float32),
    )
    if yg_pair is not None:
        d["yf"] = np.ascontiguousarray(yg_pair[:, tok])
    return d


def _back1_inputs(ins, h, yg_pair):
    tok = slice(h * 2048, (h + 1) * 2048)
    g = lambda k: np.asarray(ins["L1_" + k])
    d = dict(
        w_out=np.ascontiguousarray(g("attn_W_out"), np.float32),
        w1t=np.ascontiguousarray(g("ffn_w1_w").T, np.float32),
        w2t=np.ascontiguousarray(g("ffn_w2_w").T, np.float32),
        h1t=np.ascontiguousarray(g("ffn_h1_w").T, np.float32),
        h2t=np.ascontiguousarray(g("ffn_h2_w").T, np.float32),
        w3t=np.ascontiguousarray(g("ffn_w3_w").T, np.float32),
        s1=np.ascontiguousarray(g("ffn_w1_s").reshape(32, P).T, np.float32),
        s2=np.ascontiguousarray(g("ffn_w2_s").reshape(32, P).T, np.float32),
        sh1=np.ascontiguousarray(g("ffn_h1_s").reshape(32, P).T, np.float32),
        sh2=np.ascontiguousarray(g("ffn_h2_s").reshape(32, P).T, np.float32),
        s3=np.ascontiguousarray(g("ffn_w3_s").reshape(KD, P).T, np.float32),
    )
    if yg_pair is not None:
        d["yf"] = np.ascontiguousarray(yg_pair[:, tok])
    return d


def kernel(**ins):
    cores = list(range(8))
    x0 = np.asarray(ins["L0_attn_x"])
    nc = build_fused()
    rid = int(np.random.randint(1, 2 ** 30))
    in_maps = []
    for c in cores:
        b, h = c // 2, c % 2
        own = np.r_[h * DH:(h + 1) * DH, (1 - h) * DH:(2 - h) * DH]
        m = {}
        mf0 = _front_inputs(ins, 0, h)
        mf0["x"] = np.ascontiguousarray(x0[b].T, np.float32)
        m.update({"f0_" + k: v for k, v in mf0.items()})
        mb0 = _back0_inputs(ins, h, None)
        mb0["x"] = np.ascontiguousarray(x0[b, h * 2048:(h + 1) * 2048].T, np.float32)
        mb0["w_out"] = np.ascontiguousarray(mb0["w_out"][own])
        m.update({"b0_" + k: v for k, v in mb0.items()})
        m.update({"f1_" + k: v for k, v in _front_inputs(ins, 1, h).items()})
        mb1 = _back1_inputs(ins, h, None)
        mb1["w_out"] = np.ascontiguousarray(mb1["w_out"][own])
        m.update({"b1_" + k: v for k, v in mb1.items()})
        m["role"] = np.array([[h]], np.int32)
        m["runid"] = np.array([[rid]], np.int32)
        in_maps.append(m)
    r = run_bass_kernel_spmd(nc, in_maps, core_ids=cores)

    out = np.empty((4, 4096, 1024), np.float32)
    for c in cores:
        b, h = c // 2, c % 2
        out[b, h * 2048:(h + 1) * 2048] = np.asarray(r.results[c]["x4"]).T
    return out

